# version 05
# baseline (speedup 1.0000x reference)
from contextlib import ExitStack

import numpy as np
import concourse.bass as bass
import concourse.mybir as mybir
from concourse.bass_utils import run_bass_kernel_spmd

F32 = mybir.dt.float32
BF16 = mybir.dt.bfloat16
I32 = mybir.dt.int32
Alu = mybir.AluOpType
Act = mybir.ActivationFunctionType
AX = mybir.AxisListType.X

NCORE = 8
T = 2048
D = 2048
H = 16
HD = 128
NTC = 16
NG = 4
IF = 8192
LN2 = float(np.float32(np.log(2.0)))
INV_2LN2 = 0.7213475204444817
INV448 = 1.0 / 448.0

GSH = IF // NCORE
DSH = D // NCORE
QSH = D // NCORE
WOSH = D // NCORE
WMSH = 12288 // NCORE

N_G = GSH * D
N_U = GSH * D
N_DW = DSH * IF
N_QKV = QSH * 3 * D
N_WO = WOSH * D
OFF_G = 0
OFF_U = N_G
OFF_DW = N_G + N_U
OFF_QKV = 0
OFF_WO = N_QKV
PACK = OFF_DW + N_DW
PACKQ = N_QKV + N_WO


class Ctr:
    __slots__ = ("sem", "v")

    def __init__(self, sem):
        self.sem = sem
        self.v = 0


class Lanes:

    def __init__(self, mk, name, n):
        self.cs = [mk(f"{name}{i}") for i in range(n)]
        self.i = 0

    def nxt(self):
        c = self.cs[self.i]
        self.i = (self.i + 1) % len(self.cs)
        return c

    def cur(self):
        return [Prog.tk(c) for c in self.cs]


class Prog:
    ENGINES = ("sync", "gpsimd", "vector", "scalar", "tensor")

    def __init__(self):
        self.ops = {e: [] for e in self.ENGINES}

    def emit(self, eng, fn, waits=(), inc=None, n=1):
        waits = tuple((c, v) for (c, v) in waits if c is not None and v > 0)
        self.ops[eng].append((fn, waits, None if inc is None else inc.sem, n))
        if inc is not None:
            inc.v += n
            return (inc, inc.v)
        return None

    @staticmethod
    def tk(ctr):
        return (ctr, ctr.v)

    def materialize(self, block):
        for eng in self.ENGINES:
            ops = self.ops[eng]
            if not ops:
                continue

            def run(e, ops=ops):
                seen = {}
                for fn, waits, sem, n in ops:
                    for (ctr, val) in waits:
                        if seen.get(id(ctr), -1) < val:
                            e.wait_ge(ctr.sem, val)
                            seen[id(ctr)] = val
                    inst = fn(e)
                    if sem is not None:
                        if inst is None:
                            inst = e.nop()
                        inst.then_inc(sem, n)

            getattr(block, eng)(run)


def _bcv(t, F):
    return t[:, 0 : F // 32].rearrange("p (b e) -> p b e", e=1).broadcast_to([128, F // 32, 32])


def _blkv(t, F):
    return t[:, 0:F].rearrange("p (b e) -> p b e", e=32)



def _psb(t, a, b):
    return t[:, a : b].bitcast(BF16)[:, 0 : b - a]


def build(stage="full", debug=0):
    nc = bass.Bass("TRN2", target_bir_lowering=False, detect_race_conditions=False)

    din = lambda n, s, dt=F32: nc.dram_tensor(n, s, dt, kind="ExternalInput")
    dou = lambda n, s, dt=F32: nc.dram_tensor(n, s, dt, kind="ExternalOutput")
    dit = lambda n, s, dt=F32: nc.dram_tensor(n, s, dt)

    x_in = din("x", [T, D])
    wqkv_sh = din("wqkv_sh", [QSH, 3 * D])
    bqkv_in = din("bqkv", [1, 3 * D])
    wout_sh = din("wout_sh", [WOSH, D])
    bout_in = din("bout", [1, D])
    slopes_in = din("slopes", [1, H])
    g_sh = din("g_sh", [GSH, D])
    u_sh = din("u_sh", [GSH, D])
    dw_sh = din("dw_sh", [DSH, IF])
    c_in = din("c_in", [1, D])
    wmod_sh = din("wmod_sh", [D, WMSH])
    steps_in = din("steps", [1, NCORE])
    maskr_in = din("maskr", [1, NCORE])
    out_d = dou("out", [T, D])

    modsh_d = dit("modsh_d", [1, WMSH])
    modf_d = dit("modf_d", [1, NCORE * WMSH])
    gaeff_d = dit("gaeff_d", [1, D])
    packsh_d = dit("packsh_d", [1, PACK], BF16)
    packf_d = dit("packf_d", [NCORE, PACK], BF16)
    packqsh_d = dit("packqsh_d", [1, PACKQ], BF16)
    packqf_d = dit("packqf_d", [NCORE, PACKQ], BF16)
    dsh_d = dit("dsh_d", [H, HD, HD])
    dfull_d = dit("dfull_d", [NCORE, H, HD, HD])
    qsp_d = dit("qsp_d", [NTC, HD, D], BF16)
    ksp_d = dit("ksp_d", [NTC, HD, D], BF16)
    vsp_d = dit("vsp_d", [NTC, HD, D], BF16)
    x2sp_d = dit("x2sp_d", [T, D])
    hq_d = dit("hq_d", [IF // HD, HD, T], BF16)

    def pkflat(r, off, n):
        return packf_d[r : r + 1, off : off + n].squeeze(0)

    def pk2d(r, off, rows, cols):
        return pkflat(r, off, rows * cols).rearrange("(r c) -> r c", r=rows)

    def pkq2d(r, off, rows, cols):
        return packqf_d[r : r + 1, off : off + rows * cols].squeeze(0).rearrange(
            "(r c) -> r c", r=rows)

    def wqkv_bf(kc, c0, cn):
        return pkq2d(kc // 2, OFF_QKV + (kc % 2) * 128 * 3 * D, 128, 3 * D)[:, c0 : c0 + cn]

    def wout_bf(kc, c0, cn):
        return pkq2d(kc // 2, OFF_WO + (kc % 2) * 128 * D, 128, D)[:, c0 : c0 + cn]

    dbg = {}
    if debug >= 1:
        dbg["mod"] = dou("dbg_mod", [1, NCORE * WMSH])
        dbg["gdeq"] = dou("dbg_gdeq", [GSH, D], BF16)
        dbg["ddeq"] = dou("dbg_ddeq", [DSH, IF], BF16)
        dbg["garep"] = dou("dbg_garep", [1, D])
        dbg["gacol"] = dou("dbg_gacol", [128, 16])
        dbg["gtmp"] = dou("dbg_gtmp", [128, 16])
        dbg["gt2"] = dou("dbg_gt2", [128, 16])
        dbg["sq"] = dou("dbg_sq", [128, 16])
    if debug >= 2:
        dbg["q"] = dou("dbg_q", [NTC, HD, D], BF16)
        dbg["k"] = dou("dbg_k", [NTC, HD, D], BF16)
        dbg["v"] = dou("dbg_v", [NTC, HD, D], BF16)
        dbg["dloc"] = dou("dbg_dloc", [H, HD, HD])
        dbg["sinit"] = dou("dbg_sinit", [H, HD, HD])
    if debug >= 3:
        dbg["x2"] = dou("dbg_x2", [T, D])
        dbg["x2q"] = dou("dbg_x2q", [16, HD, T], BF16)
        dbg["h2"] = dou("dbg_h2", [T, D])
        dbg["eln"] = dou("dbg_eln", [NTC, 128, 2])
    if debug >= 4:
        dbg["hq"] = dou("dbg_hq", [IF // HD, HD, T], BF16)
    if debug >= 5:
        dbg["ey"] = dou("dbg_ey", [T, D])
        dbg["ee"] = dou("dbg_ee", [T, D], I32)
        dbg["eu"] = dou("dbg_eu", [T, D])
        dbg["ez"] = dou("dbg_ez", [T, D])
        dbg["er"] = dou("dbg_er", [T, D], I32)
        dbg["es"] = dou("dbg_es", [NTC, 128, 64])

    p = Prog()
    E = p.emit
    TK = p.tk

    with ExitStack() as top:
        sb = lambda n, sh, dt=F32: top.enter_context(nc.sbuf_tensor(n, sh, dt))
        mkc = lambda n: Ctr(top.enter_context(nc.semaphore(n)))

        QL = Lanes(mkc, "cQ", 6)
        GL = Lanes(mkc, "cG", 6)
        cGc = mkc("cGc")
        cV = mkc("cV")
        cS = mkc("cS")
        cT = mkc("cT")
        cO = mkc("cO")
        cOg = mkc("cOg")
        cC = mkc("cC")
        ALLC = tuple(QL.cs) + tuple(GL.cs) + (cGc, cV, cS, cT, cO, cOg, cC)

        def bar():
            return [TK(c) for c in ALLC]

        cp = sb("cpool", [128, 8])
        CLN2, CNLN2, CNB3, CPB3, CEPS = 0, 1, 2, 3, 4
        ident_f = sb("ident_f", [128, 128])
        ident_b = sb("ident_b", [128, 128], BF16)
        iota_d = sb("iota_d", [128, 128])
        slopes_rep = sb("slopes_rep", [128, H])
        nslopes = sb("nslopes", [128, H])
        maskT = sb("maskT", [128, H * 128], BF16)
        qdec = sb("qdec", [128, H * 128], BF16)
        kdcol = sb("kdcol", [128, H])
        bdcol = sb("bdcol", [128, H])
        kd2col = sb("kd2col", [128, NTC * H])
        ca1c = sb("ca1c", [128, 16])
        sac = sb("sac", [128, 16])
        gbcol = sb("gbcol", [128, 16])
        ga_rep = sb("ga_rep", [128, D], BF16)
        cb1_rep = sb("cb1_rep", [128, D], BF16)
        sb_rep = sb("sb_rep", [128, D], BF16)
        ones_row = sb("ones_row", [1, 512])
        bout_row = sb("bout_row", [1, D])
        coefs = sb("coefs", [128, NCORE * H])
        colp = sb("colp", [128, 2])

        with ExitStack() as pa:
            asb = lambda n, sh, dt=F32: pa.enter_context(nc.sbuf_tensor(n, sh, dt))
            scmod = asb("scmod", [128, 16])
            stepr = asb("stepr", [128, NCORE])
            maskrr = asb("maskrr", [128, NCORE])
            graw = asb("graw", [128, 16])
            gbraw = asb("gbraw", [128, 16])
            gtmp = asb("gtmp", [128, 16])
            mtmp = asb("mtmp", [128, H * 128])
            mmin = asb("mmin", [128, 128])
            wmt = [asb(f"wmt{i}", [128, WMSH]) for i in range(2)]
            modrow = asb("modrow", [1, WMSH])

            t_c = E("sync", lambda e: e.dma_start(
                out=scmod[:, :], in_=c_in[0:1, :].squeeze(0).rearrange("(k q) -> q k", q=128),
                allow_slow_non_contiguous=True),
              inc=QL.nxt(), n=16)
            t_bout = E("sync", lambda e: e.dma_start(out=bout_row[:, :], in_=bout_in[:, :]),
                       inc=QL.nxt(), n=16)

            for col, val in ((CLN2, LN2), (CNLN2, -LN2), (CNB3, -3.0 * LN2),
                             (CPB3, 3.0 * LN2), (CEPS, 1e-5)):
                E("gpsimd", lambda e, col=col, val=val: e.memset(cp[:, col : col + 1], val),
                  inc=cGc, n=1)
            E("gpsimd", lambda e: e.memset(ones_row[:, :], 1.0), inc=cGc, n=1)
            t_ones = TK(cGc)
            E("gpsimd", lambda e: e.iota(iota_d[:, :], [[1, 128]], base=0, channel_multiplier=-1,
                                         allow_small_or_imprecise_dtypes=True), inc=cGc, n=1)
            t_iota = TK(cGc)
            t_sl = E("gpsimd", lambda e: e.dma_start(
                out=slopes_rep[:, :], in_=slopes_in[0:1, :].partition_broadcast(128).squeeze(1)),
              inc=GL.nxt(), n=16)
            t_step1 = E("gpsimd", lambda e: e.dma_start(
                out=stepr[:, :], in_=steps_in[0:1, :].partition_broadcast(128).squeeze(1)),
              inc=GL.nxt(), n=16)
            t_step2 = E("gpsimd", lambda e: e.dma_start(
                out=maskrr[:, :], in_=maskr_in[0:1, :].partition_broadcast(128).squeeze(1)),
              inc=GL.nxt(), n=16)
            t_steps = [t_step1, t_step2]

            E("gpsimd", lambda e: e.memset(mtmp[:, 0:128], 1.0), inc=cGc, n=1)
            t_ones128 = TK(cGc)
            E("gpsimd", lambda e: e.affine_select(ident_f[:, :], mtmp[:, 0:128], [[1, 128]],
                                                  compare_op=Alu.is_equal, fill=0.0, base=0,
                                                  channel_multiplier=-1),
              waits=[t_ones128], inc=cGc, n=1)
            t_identg = TK(cGc)
            E("vector", lambda e: e.tensor_copy(ident_b[:, :], ident_f[:, :]),
              waits=[t_identg], inc=cV, n=1)
            E("vector", lambda e: e.tensor_copy(ident_f[:, :], ident_b[:, :]), inc=cV, n=1)
            t_ident = TK(cV)
            E("vector", lambda e: e.tensor_scalar(nslopes[:, :], slopes_rep[:, :], -1.0, None,
                                                  Alu.mult), waits=[t_sl], inc=cV, n=1)
            t_nsl = TK(cV)
            E("vector", lambda e: e.tensor_scalar(colp[:, 0:1], iota_d[:, 0:1], -1.0, None,
                                                  Alu.mult), waits=[t_iota], inc=cV, n=1)
            t_colp = TK(cV)

            E("scalar", lambda e: e.activation(scmod[:, :], scmod[:, :], Act.Silu),
              waits=[t_c], inc=cS, n=1)
            t_scmod = TK(cS)

            with nc.psum_tensor("pmod", [1, 3 * 512], F32) as pmod:
                mm_t = []
                for kc in range(16):
                    w = [mm_t[kc - 2]] if kc >= 2 else []
                    t_wm = E("sync", lambda e, kc=kc: e.dma_start(
                        out=wmt[kc % 2][:, :], in_=wmod_sh[kc * 128 : (kc + 1) * 128, :]),
                      waits=w, inc=QL.nxt(), n=16)

                    def mmf(e, kc=kc):
                        r = None
                        for n3 in range(3):
                            r = e.matmul(pmod[0:1, n3 * 512 : (n3 + 1) * 512],
                                         scmod[:, kc : kc + 1],
                                         wmt[kc % 2][:, n3 * 512 : (n3 + 1) * 512],
                                         start=(kc == 0), stop=(kc == 15))
                        return r
                    E("tensor", mmf, waits=[t_wm, t_scmod], inc=cT, n=1)
                    mm_t.append(TK(cT))
                E("vector", lambda e: e.tensor_copy(modrow[:, :], pmod[0:1, 0:WMSH]),
                  waits=[mm_t[-1]], inc=cV, n=1)
                t_modrow = TK(cV)
            t_modst = E("gpsimd", lambda e: e.dma_start(out=modsh_d[:, :], in_=modrow[:, :]),
                        waits=[t_modrow], inc=GL.nxt(), n=16)
            E("gpsimd", lambda e: e.collective_compute(
                "AllGather", Alu.bypass, replica_groups=[list(range(NCORE))],
                ins=[modsh_d.ap().opt()], outs=[modf_d.ap().opt()]),
              waits=[t_modst], inc=cC, n=1)
            t_modag = TK(cC)
            if debug >= 1:
                E("gpsimd", lambda e: e.dma_start(out=dbg["mod"][:, :], in_=modf_d[:, :]),
                  waits=[t_modag], inc=cOg, n=16)

            def modcol(dst, off):
                return E("gpsimd", lambda e, dst=dst, off=off: e.dma_start(
                    out=dst,
                    in_=modf_d[0:1, off : off + D].squeeze(0).rearrange("(k q) -> q k", q=128),
                    allow_slow_non_contiguous=True),
                  waits=[t_modag], inc=GL.nxt(), n=16)

            t_modcols = []
            t_modcols.append(modcol(sac[:, :], 0))
            t_modcols.append(modcol(ca1c[:, :], D))
            t_modcols.append(modcol(graw[:, :], 2 * D))
            t_modcols.append(modcol(gbraw[:, :], 5 * D))
            t_modcols.append(E("gpsimd", lambda e: e.dma_start(
                out=cb1_rep[:, :],
                in_=modf_d[0:1, 4 * D : 5 * D].partition_broadcast(128).squeeze(1)),
              waits=[t_modag], inc=GL.nxt(), n=16))
            t_modcols.append(E("gpsimd", lambda e: e.dma_start(
                out=sb_rep[:, :],
                in_=modf_d[0:1, 3 * D : 4 * D].partition_broadcast(128).squeeze(1)),
              waits=[t_modag], inc=GL.nxt(), n=16))

            E("vector", lambda e: e.tensor_scalar(ca1c[:, :], ca1c[:, :], 1.0, None, Alu.add),
              waits=t_modcols, inc=cV, n=1)
            E("vector", lambda e: e.tensor_scalar(cb1_rep[:, :], cb1_rep[:, :], 1.0, None, Alu.add),
              inc=cV, n=1)
            t_mods = TK(cV)

            gt2 = scmod
            for raw, dstc in ((graw, graw), (gbraw, gbcol)):
                E("scalar", lambda e, raw=raw: e.activation(gtmp[:, :], raw[:, :], Act.Abs),
                  waits=t_modcols + [mm_t[-1], TK(cV)], inc=cS, n=1)
                E("scalar", lambda e, raw=raw: e.activation(gt2[:, :], raw[:, :], Act.Sign),
                  inc=cS, n=1)
                E("scalar", lambda e, raw=raw: e.activation(raw[:, :], gt2[:, :], Act.Square),
                  inc=cS, n=1)
                t_abs = TK(cS)
                if debug >= 1 and raw is graw:
                    E("gpsimd", lambda e: e.dma_start(out=dbg["gtmp"][:, :], in_=gtmp[:, :]),
                      waits=[t_abs], inc=cOg, n=16)
                    E("gpsimd", lambda e: e.dma_start(out=dbg["gt2"][:, :], in_=gt2[:, :]),
                      waits=[t_abs], inc=cOg, n=16)
                    E("gpsimd", lambda e: e.dma_start(out=dbg["sq"][:, :], in_=graw[:, :]),
                      waits=[t_abs], inc=cOg, n=16)
                E("vector", lambda e: e.tensor_scalar(gtmp[:, :], gtmp[:, :], 0.25, None, Alu.max),
                  waits=[t_abs], inc=cV, n=1)
                E("vector", lambda e, raw=raw: e.tensor_scalar(raw[:, :], raw[:, :], -1.0, 1.0,
                                                               Alu.mult, Alu.add),
                  waits=[TK(cV)], inc=cV, n=1)
                E("vector", lambda e, raw=raw: e.tensor_tensor(raw[:, :], raw[:, :], gt2[:, :],
                                                               Alu.add), waits=[TK(cV)], inc=cV, n=1)
                E("vector", lambda e, raw=raw, dstc=dstc: e.tensor_tensor(
                    dstc[:, :], raw[:, :], gtmp[:, :], Alu.mult), waits=[TK(cV)], inc=cV, n=1)
            t_geff = TK(cV)

            t_gast = E("gpsimd", lambda e: e.dma_start(
                out=gaeff_d[0:1, :].squeeze(0).rearrange("(k q) -> q k", q=128), in_=graw[:, :],
                allow_slow_non_contiguous=True),
              waits=[t_geff], inc=GL.nxt(), n=16)
            if debug >= 1:
                E("gpsimd", lambda e: e.dma_start(out=dbg["gacol"][:, :], in_=graw[:, :]),
                  waits=[t_geff], inc=cOg, n=16)
            t_garep = E("gpsimd", lambda e: e.dma_start(
                out=ga_rep[:, :], in_=gaeff_d[0:1, :].partition_broadcast(128).squeeze(1)),
              waits=[t_gast], inc=GL.nxt(), n=16)
            if debug >= 1:
                E("gpsimd", lambda e: e.dma_start(out=dbg["garep"][:, :], in_=gaeff_d[:, :]),
                  waits=[t_gast], inc=cOg, n=16)

            E("vector", lambda e: e.tensor_scalar(mmin[:, :], iota_d[:, :], 0.0, 10000.0,
                                                  Alu.min, Alu.mult),
              waits=[t_iota, t_geff], inc=cV, n=1)
            t_mneg = TK(cV)
            for h in range(H):
                E("vector", lambda e, h=h: e.scalar_tensor_tensor(
                    mtmp[:, h * 128 : (h + 1) * 128], iota_d[:, :],
                    nslopes[:, h : h + 1], mmin[:, :], Alu.mult, Alu.add),
                  waits=[t_mneg, t_nsl], inc=cV, n=1)
            t_m1 = TK(cV)
            E("scalar", lambda e: e.activation(maskT[:, :], mtmp[:, :], Act.Exp),
              waits=[t_m1], inc=cS, n=1)
            t_maskT = TK(cS)
            for h in range(H):
                E("vector", lambda e, h=h: e.scalar_tensor_tensor(
                    mtmp[:, h * 128 : (h + 1) * 128], iota_d[:, :], 1.0,
                    colp[:, 0:1].broadcast_to([128, 1, 128]).squeeze(1), Alu.mult, Alu.add),
                  waits=[t_colp, t_maskT], inc=cV, n=1)
                E("vector", lambda e, h=h: e.tensor_scalar(
                    mtmp[:, h * 128 : (h + 1) * 128], mtmp[:, h * 128 : (h + 1) * 128],
                    nslopes[:, h : h + 1], None, Alu.mult), inc=cV, n=1)
            t_q1 = TK(cV)
            E("scalar", lambda e: e.activation(qdec[:, :], mtmp[:, :], Act.Exp),
              waits=[t_q1], inc=cS, n=1)
            t_qdec = TK(cS)
            for h in range(H):
                E("vector", lambda e, h=h: e.tensor_scalar(kdcol[:, h : h + 1], colp[:, 0:1],
                                                           -128.0, None, Alu.add),
                  waits=[t_colp, TK(cV)], inc=cV, n=1)
                E("vector", lambda e, h=h: e.tensor_scalar(kdcol[:, h : h + 1],
                                                           kdcol[:, h : h + 1],
                                                           slopes_rep[:, h : h + 1], None,
                                                           Alu.mult), waits=[TK(cV)], inc=cV, n=1)
            E("vector", lambda e: e.tensor_scalar(bdcol[:, :], slopes_rep[:, :], -128.0, None,
                                                  Alu.mult), waits=[t_sl], inc=cV, n=1)
            for tc in range(NTC):
                E("vector", lambda e, tc=tc: e.tensor_scalar(colp[:, 1:2], colp[:, 0:1],
                                                             float(128 * tc - T), None, Alu.add),
                  waits=[TK(cV)], inc=cV, n=1)
                t_c12 = TK(cV)
                for h in range(H):
                    E("vector", lambda e, tc=tc, h=h: e.tensor_scalar(
                        kd2col[:, tc * H + h : tc * H + h + 1], colp[:, 1:2],
                        slopes_rep[:, h : h + 1], None, Alu.mult),
                      waits=[t_c12], inc=cV, n=1)
            E("vector", lambda e: e.tensor_scalar(gtmp[:, 0:H], slopes_rep[:, :], -2048.0, None,
                                                  Alu.mult), waits=[TK(cV)], inc=cV, n=1)
            t_ns2048 = TK(cV)
            for j in range(NCORE):
                E("vector", lambda e, j=j: e.tensor_tensor(
                    coefs[:, j * H : (j + 1) * H], gtmp[:, 0:H],
                    stepr[:, j : j + 1].broadcast_to([128, 1, H]).squeeze(1), Alu.mult),
                  waits=t_steps + [t_ns2048], inc=cV, n=1)
            t_dec1 = TK(cV)
            E("scalar", lambda e: e.activation(kdcol[:, :], kdcol[:, :], Act.Exp),
              waits=[t_dec1], inc=cS, n=1)
            E("scalar", lambda e: e.activation(bdcol[:, :], bdcol[:, :], Act.Exp), inc=cS, n=1)
            E("scalar", lambda e: e.activation(kd2col[:, :], kd2col[:, :], Act.Exp), inc=cS, n=1)
            E("scalar", lambda e: e.activation(coefs[:, :], coefs[:, :], Act.Exp), inc=cS, n=1)
            t_dec2 = TK(cS)
            for j in range(NCORE):
                E("vector", lambda e, j=j: e.tensor_tensor(
                    coefs[:, j * H : (j + 1) * H], coefs[:, j * H : (j + 1) * H],
                    maskrr[:, j : j + 1].broadcast_to([128, 1, H]).squeeze(1), Alu.mult),
                  waits=[t_dec2], inc=cV, n=1)
            t_coefs = TK(cV)
            barA = bar()

        with ExitStack() as pb:
            bsb = lambda n, sh, dt=F32: pb.enter_context(nc.sbuf_tensor(n, sh, dt))
            win = [bsb(f"bwin{i}", [128, D]) for i in range(2)]
            bA = bsb("bA", [128, D])
            bB = bsb("bB", [128, D])
            bC = bsb("bC", [128, D])
            bEi = bsb("bEi", [128, D], I32)
            bdq = [bsb(f"bdq{i}", [128, D], BF16) for i in range(2)]
            bmx = bsb("bmx", [128, D // 32])
            bs_ = bsb("bs", [128, D // 32])
            brs = bsb("brs", [128, D // 32])

            jobs = []
            for i in range(GSH // 128):
                jobs.append((g_sh, i * 128, 0, D, OFF_G + i * 128 * D))
            for i in range(GSH // 128):
                jobs.append((u_sh, i * 128, 0, D, OFF_U + i * 128 * D))
            for i in range(DSH // 128):
                for cchunk in range(IF // D):
                    jobs.append((dw_sh, i * 128, cchunk * D, IF,
                                 OFF_DW + i * 128 * IF))

            if debug >= 1:
                dbg["win0"] = dou("dbg_win0", [128, D])

            cjobs = []
            for i in range(QSH // 128):
                cjobs.append((wqkv_sh, i * 128, 3 * D, OFF_QKV + i * 128 * 3 * D))
            for i in range(WOSH // 128):
                cjobs.append((wout_sh, i * 128, D, OFF_WO + i * 128 * D))
            for (src, r0, F, off) in cjobs:
                E("gpsimd", lambda e, src=src, r0=r0, F=F, off=off: e.dma_start(
                    out=packqsh_d[0:1, off : off + 128 * F].squeeze(0).rearrange(
                        "(q f) -> q f", q=128),
                    in_=src[r0 : r0 + 128, :]), inc=GL.nxt(), n=16)
            E("gpsimd", lambda e: e.collective_compute(
                "AllGather", Alu.bypass, replica_groups=[list(range(NCORE))],
                ins=[packqsh_d.ap().opt()], outs=[packqf_d.ap().opt()]),
              waits=GL.cur(), inc=cC, n=1)
            t_packagQ = TK(cC)

            st_t = {}
            for j, (src, r0, c0, rowlen, off) in enumerate(jobs):
                F = D
                w = list(barA) if j < 2 else [st_t[j - 2]]
                t_ld = E("sync", lambda e, src=src, r0=r0, c0=c0, j=j: e.dma_start(
                    out=win[j % 2][:, 0:D], in_=src[r0 : r0 + 128, c0 : c0 + D]),
                  waits=w, inc=QL.nxt(), n=16)
                if j == 0 and debug >= 1:
                    E("gpsimd", lambda e: e.dma_start(out=dbg["win0"][:, :], in_=win[0][:, :]),
                      waits=[t_ld], inc=cOg, n=16)
                wt = win[j % 2]
                NB = F // 32
                E("vector", lambda e, wt=wt, F=F, NB=NB: e.tensor_reduce(
                    bmx[:, 0:NB].rearrange("p (b e) -> p b e", e=1), _blkv(wt, F), AX, Alu.max,
                    apply_absolute_value=True), waits=[t_ld] + list(barA), inc=cV, n=1)
                E("vector", lambda e, NB=NB: e.tensor_scalar(bs_[:, 0:NB], bmx[:, 0:NB], INV448,
                                                             1e-30, Alu.mult, Alu.max),
                  waits=[TK(cV)], inc=cV, n=1)
                E("vector", lambda e, NB=NB: e.reciprocal(brs[:, 0:NB], bs_[:, 0:NB]),
                  waits=[TK(cV)], inc=cV, n=1)
                E("vector", lambda e, wt=wt, F=F: e.tensor_tensor(
                    _blkv(bA, F), _blkv(wt, F), _bcv(brs, F), Alu.mult),
                  waits=[TK(cV)], inc=cV, n=1)
                t_y = TK(cV)
                E("scalar", lambda e, F=F: e.activation(bB[:, 0:F], bA[:, 0:F], Act.Square),
                  waits=[t_y] + list(barA), inc=cS, n=1)
                t_sq = TK(cS)
                E("vector", lambda e, F=F: e.tensor_scalar(bB[:, 0:F], bB[:, 0:F], 0.000244140625,
                                                           None, Alu.max), waits=[t_sq],
                  inc=cV, n=1)
                t_cl = TK(cV)
                E("scalar", lambda e, F=F: e.activation(bB[:, 0:F], bB[:, 0:F], Act.Ln),
                  waits=[t_cl], inc=cS, n=1)
                t_ln = TK(cS)
                E("vector", lambda e, F=F: e.tensor_scalar(bEi[:, 0:F], bB[:, 0:F], INV_2LN2,
                                                           -0.5, Alu.mult, Alu.add),
                  waits=[t_ln], inc=cV, n=1)
                t_ei = TK(cV)
                E("scalar", lambda e, F=F: e.activation(bB[:, 0:F], bEi[:, 0:F], Act.Exp,
                                                        bias=cp[:, CNB3 : CNB3 + 1],
                                                        scale=cp[:, CLN2 : CLN2 + 1]),
                  waits=[t_ei], inc=cS, n=1)
                E("scalar", lambda e, F=F: e.activation(bC[:, 0:F], bEi[:, 0:F], Act.Exp,
                                                        bias=cp[:, CPB3 : CPB3 + 1],
                                                        scale=cp[:, CNLN2 : CNLN2 + 1]),
                  inc=cS, n=1)
                t_exp = TK(cS)
                E("vector", lambda e, F=F: e.tensor_tensor(bC[:, 0:F], bA[:, 0:F], bC[:, 0:F],
                                                           Alu.mult), waits=[t_exp], inc=cV, n=1)
                E("vector", lambda e, F=F: e.tensor_copy(bEi[:, 0:F], bC[:, 0:F]), inc=cV, n=1)
                E("vector", lambda e, F=F: e.tensor_tensor(bC[:, 0:F], bEi[:, 0:F], bB[:, 0:F],
                                                           Alu.mult), inc=cV, n=1)
                wdq = [st_t[j - 2]] if j >= 2 else []
                E("vector", lambda e, F=F, j=j: e.tensor_tensor(_blkv(bdq[j % 2], F), _blkv(bC, F),
                                                           _bcv(bs_, F), Alu.mult),
                  waits=wdq, inc=cV, n=1)
                t_dq = TK(cV)
                st_t[j] = E("gpsimd", lambda e, off=off, c0=c0, rowlen=rowlen, j=j: e.dma_start(
                    out=packsh_d[0:1, off : off + 128 * rowlen].squeeze(0).rearrange(
                        "(q f) -> q f", q=128)[:, c0 : c0 + D],
                    in_=bdq[j % 2][:, 0:D]), waits=[t_dq], inc=GL.nxt(), n=16)

            barB = bar()
            E("gpsimd", lambda e: e.collective_compute(
                "AllGather", Alu.bypass, replica_groups=[list(range(NCORE))],
                ins=[packsh_d.ap().opt()], outs=[packf_d.ap().opt()]),
              waits=GL.cur(), inc=cC, n=1)
            t_packag = TK(cC)

        if debug >= 1:
            E("gpsimd", lambda e: e.dma_start(out=dbg["gdeq"][:, :], in_=pk2d(0, OFF_G, GSH, D)),
              waits=[t_packag], inc=cOg, n=16)
            E("gpsimd", lambda e: e.dma_start(out=dbg["ddeq"][:, :], in_=pk2d(0, OFF_DW, DSH, IF)),
              waits=[t_packag], inc=cOg, n=16)

        def finish(extra):
            E("sync", lambda e: None, waits=list(extra) + bar())
            blk = top.enter_context(nc.Block("all"))
            p.materialize(blk)

        if stage == "B":
            E("sync", lambda e: e.dma_start(out=out_d[:, :], in_=x_in[:, :]),
              waits=[t_packag], inc=cO, n=16)
            finish([])
            return nc

        t_spq = {}
        t_spk = {}
        t_spv = {}
        t_dag = None
        t_x2sp = {}
        t_x2q_last = None

        with ExitStack() as xs:
            x2q = xs.enter_context(nc.sbuf_tensor("x2q", [128, 16 * T], BF16))

            def x2q_sl(kc, a, b):
                return x2q[:, kc * T + a : kc * T + b]

            ccut = 5
            if stage.startswith("C") and len(stage) > 1:
                ccut = int(stage[1:])
            c_ng = (0 if ccut == 0 else 1) if ccut <= 3 else NG
            with ExitStack() as pc:
                csb = lambda n, sh, dt=F32: pc.enter_context(nc.sbuf_tensor(n, sh, dt))
                xt = csb("cxt", [128, D])
                sqs = csb("csqs", [128, D])
                h1 = csb("ch1", [128, 16 * 512], BF16)
                wq = [csb(f"cwq{i}", [128, 1024], BF16) for i in range(2)]
                kg = csb("ckg", [128, H * 512], BF16)
                vgf = csb("cvgf", [128, H * 512], BF16)
                vgt = csb("cvgt", [128, NG * D], BF16)
                stg = [csb(f"cstg{i}", [128, 512], BF16) for i in range(2)]
                kd2t = [csb(f"ckd2t{i}", [128, 128], BF16) for i in range(2)]
                Dsb = csb("cD", [128, H * 128])
                lnm = csb("clnm", [128, 1])
                lnr = csb("clnr", [128, 1])
                lna = csb("clna", [128, 1])
                lnb = csb("clnb", [128, 1])
                bqr = csb("cbqr", [1, 1024])
                pqk = pc.enter_context(nc.psum_tensor("pqk", [128, 8 * 512], F32))
                ptr = pqk
                pD_base = 4 * 512

                E("vector", lambda e: e.memset(Dsb[:, :], 0.0), waits=barB, inc=cV, n=1)
                t_Dadd = TK(cV)

                tr_free = [None] * 4
                t_x_free = None
                pqk_free = [None] * 8
                wq_use = [None, None]
                bqr_use = None
                kg_free = None
                vgf_free = None
                vgt_free = [None] * NG
                stg_free = [None, None]
                kd2_free = [None, None]
                pD_free = [None] * 4
                first_sync = barB
                first_sc = barB

                for g in range(c_ng):
                    qspl = []
                    kspl = []
                    h1_last = None
                    for tc4 in range(4):
                        tcg = g * 4 + tc4
                        w = ([t_x_free] if t_x_free else []) + (first_sync or [])
                        t_x = E("sync", lambda e, tcg=tcg: e.dma_start(
                            out=xt[:, :], in_=x_in[tcg * 128 : (tcg + 1) * 128, :]),
                          waits=w, inc=QL.nxt(), n=16)
                        first_sync = None
                        E("vector", lambda e: e.reduce_sum(lna[:, :], xt[:, :], AX),
                          waits=[t_x], inc=cV, n=1)
                        t_sx = TK(cV)
                        E("scalar", lambda e: e.activation(sqs[:, :], xt[:, :], Act.Square,
                                                           accum_out=lnb[:, :]),
                          waits=[t_x] + (first_sc or []), inc=cS, n=1)
                        first_sc = None
                        t_sq2 = TK(cS)
                        E("vector", lambda e: e.tensor_scalar(lnm[:, :], lna[:, :], 1.0 / D, None,
                                                              Alu.mult), waits=[t_sx, TK(cV)],
                          inc=cV, n=1)
                        E("vector", lambda e: e.tensor_scalar(lnb[:, :], lnb[:, :], 1.0 / D, None,
                                                              Alu.mult), waits=[t_sq2, TK(cV)],
                          inc=cV, n=1)
                        E("vector", lambda e: e.scalar_tensor_tensor(
                            lna[:, :], lnm[:, :], lnm[:, 0:1], lnb[:, :], Alu.mult, Alu.subtract),
                          waits=[TK(cV)], inc=cV, n=1)
                        E("vector", lambda e: e.tensor_scalar(lna[:, :], lna[:, :], -1.0, None,
                                                              Alu.mult), waits=[TK(cV)], inc=cV, n=1)
                        t_var = TK(cV)
                        E("scalar", lambda e: e.activation(lna[:, :], lna[:, :], Act.Sqrt,
                                                           bias=cp[:, CEPS : CEPS + 1]),
                          waits=[t_var], inc=cS, n=1)
                        t_std = TK(cS)
                        E("vector", lambda e: e.reciprocal(lnr[:, :], lna[:, :]),
                          waits=[t_std], inc=cV, n=1)
                        E("vector", lambda e: e.tensor_scalar(xt[:, :], xt[:, :], lnm[:, 0:1],
                                                              lnr[:, 0:1], Alu.subtract,
                                                              Alu.mult), waits=[TK(cV)],
                          inc=cV, n=1)
                        t_xc = TK(cV)
                        for m in range(16):
                            s4 = m % 4
                            w = [t_xc, t_ident] + ([tr_free[s4]] if tr_free[s4] else [])
                            E("tensor", lambda e, m=m, s4=s4: e.transpose(
                                ptr[:, s4 * 512 : s4 * 512 + 128],
                                xt[:, m * 128 : (m + 1) * 128], ident_f[:, :]),
                              waits=w, inc=cT, n=1)
                            t_tr = TK(cT)
                            E("vector", lambda e, m=m, s4=s4, tc4=tc4: e.tensor_scalar(
                                h1[:, m * 512 + tc4 * 128 : m * 512 + (tc4 + 1) * 128],
                                ptr[:, s4 * 512 : s4 * 512 + 128],
                                ca1c[:, m : m + 1], sac[:, m : m + 1], Alu.mult, Alu.add),
                              waits=[t_tr, t_mods], inc=cV, n=1)
                            tr_free[s4] = TK(cV)
                        t_x_free = tr_free[3]
                        h1_last = tr_free[3]
                    t_h1 = h1_last
                    if ccut == 1:
                        continue

                    for mb in range(6):
                        w = ([bqr_use] if bqr_use else [])
                        t_bq = E("sync", lambda e, mb=mb: e.dma_start(
                            out=bqr[:, :], in_=bqkv_in[0:1, mb * 1024 : (mb + 1) * 1024]),
                          waits=w, inc=QL.nxt(), n=16)

                        def qkinit(e, mb=mb):
                            r = None
                            for i8 in range(8):
                                r = e.matmul(pqk[:, i8 * 512 : (i8 + 1) * 512],
                                             bqr[0:1, i8 * 128 : (i8 + 1) * 128],
                                             ones_row[0:1, 0:512],
                                             start=True, stop=False)
                            return r
                        w = [t_bq, t_ones] + [t for t in pqk_free if t] + \
                            [t for t in tr_free if t] + [t for t in pD_free if t]
                        E("tensor", qkinit, waits=w, inc=cT, n=1)
                        bqr_use = TK(cT)
                        for kc in range(16):
                            t_wq = E("sync", lambda e, kc=kc, mb=mb: e.dma_start(
                                out=wq[kc % 2][:, :], in_=wqkv_bf(kc, mb * 1024, 1024)),
                              waits=[t_packagQ] + ([wq_use[kc % 2]] if wq_use[kc % 2] else []),
                              inc=QL.nxt(), n=16)

                            def qkmm(e, kc=kc):
                                r = None
                                for i8 in range(8):
                                    r = e.matmul(pqk[:, i8 * 512 : (i8 + 1) * 512],
                                                 wq[kc % 2][:, i8 * 128 : (i8 + 1) * 128],
                                                 h1[:, kc * 512 : (kc + 1) * 512],
                                                 start=False, stop=(kc == 15))
                                return r
                            E("tensor", qkmm, waits=[t_wq, t_h1], inc=cT, n=1)
                            wq_use[kc % 2] = TK(cT)
                        t_stop = TK(cT)
                        for i8 in range(8):
                            m = mb * 8 + i8
                            if m < 16:
                                s2 = i8 % 2
                                w = [t_stop] + ([stg_free[s2]] if stg_free[s2] else [])
                                E("vector", lambda e, i8=i8, s2=s2: e.tensor_copy(
                                    stg[s2][:, :], pqk[:, i8 * 512 : (i8 + 1) * 512]),
                                  waits=w, inc=cV, n=1)
                                t_ev = TK(cV)
                                pqk_free[i8] = t_ev
                                stg_free[s2] = E(
                                    "gpsimd", lambda e, g=g, m=m, s2=s2: e.dma_start(
                                        out=qsp_d[g * 4 : (g + 1) * 4, :,
                                                  m * 128 : (m + 1) * 128]
                                        .rearrange("t p j -> p t j"),
                                        in_=stg[s2][:, :].rearrange("p (t j) -> p t j", t=4)),
                                    waits=[t_ev], inc=GL.nxt(), n=16)
                                qspl.append(stg_free[s2])
                            elif m < 32:
                                mh = m - 16
                                w = [t_stop] + (kg_free or []) + \
                                    ([vgf_free] if vgf_free else [])
                                E("vector", lambda e, i8=i8, mh=mh: e.tensor_copy(
                                    kg[:, mh * 512 : (mh + 1) * 512],
                                    pqk[:, i8 * 512 : (i8 + 1) * 512]), waits=w, inc=cV, n=1)
                                pqk_free[i8] = TK(cV)
                                kspl.append(E(
                                    "gpsimd", lambda e, g=g, mh=mh: e.dma_start(
                                        out=ksp_d[g * 4 : (g + 1) * 4, :,
                                                  mh * 128 : (mh + 1) * 128]
                                        .rearrange("t p j -> p t j"),
                                        in_=kg[:, mh * 512 : (mh + 1) * 512].rearrange(
                                            "p (t j) -> p t j", t=4)),
                                    waits=[pqk_free[i8]], inc=GL.nxt(), n=16))
                            else:
                                mv = m - 32
                                w = [t_stop] + ([vgf_free] if vgf_free else [])
                                E("vector", lambda e, i8=i8, mv=mv: e.tensor_copy(
                                    vgf[:, mv * 512 : (mv + 1) * 512],
                                    pqk[:, i8 * 512 : (i8 + 1) * 512]), waits=w, inc=cV, n=1)
                                pqk_free[i8] = TK(cV)
                    t_kg = TK(cV)
                    for tcg in range(g * 4, (g + 1) * 4):
                        t_spq[tcg] = qspl
                        t_spk[tcg] = kspl
                    if ccut == 2:
                        continue

                    for h in range(H):
                        t_dmm = None
                        for tc4 in range(4):
                            tcg = g * 4 + tc4
                            sv = tc4 % 4
                            w = [t_kg, t_ident] + ([tr_free[sv]] if tr_free[sv] else [])
                            E("tensor", lambda e, h=h, tc4=tc4, sv=sv: e.transpose(
                                _psb(ptr, sv * 512, sv * 512 + 128),
                                vgf[:, h * 512 + tc4 * 128 : h * 512 + (tc4 + 1) * 128],
                                ident_b[:, :]), waits=w, inc=cT, n=1)
                            t_vtr = TK(cT)
                            w = [t_vtr] + ([vgt_free[tc4]] if vgt_free[tc4] else [])
                            E("vector", lambda e, h=h, tc4=tc4, sv=sv: e.tensor_copy(
                                vgt[:, tc4 * D + h * 128 : tc4 * D + (h + 1) * 128],
                                _psb(ptr, sv * 512, sv * 512 + 128)), waits=w, inc=cV, n=1)
                            t_vt = TK(cV)
                            tr_free[sv] = t_vt
                            s2 = tc4 % 2
                            w = [t_kg, t_ident] + ([tr_free[sv]] if tr_free[sv] else [])
                            E("tensor", lambda e, h=h, tc4=tc4, sv=sv: e.transpose(
                                _psb(ptr, sv * 512, sv * 512 + 128),
                                kg[:, h * 512 + tc4 * 128 : h * 512 + (tc4 + 1) * 128],
                                ident_b[:, :]), waits=w, inc=cT, n=1)
                            t_ktr = TK(cT)
                            w = [t_ktr, t_dec2] + ([kd2_free[s2]] if kd2_free[s2] else [])
                            E("vector", lambda e, h=h, tcg=tcg, sv=sv, s2=s2: e.tensor_scalar(
                                kd2t[s2][:, :], _psb(ptr, sv * 512, sv * 512 + 128),
                                kd2col[:, tcg * H + h : tcg * H + h + 1], None, Alu.mult),
                              waits=w, inc=cV, n=1)
                            t_kd2 = TK(cV)
                            tr_free[sv] = t_kd2
                            sD = h % 4
                            w = [t_kd2, t_vt]
                            if tc4 == 0 and pD_free[sD]:
                                w.append(pD_free[sD])
                            E("tensor", lambda e, h=h, tc4=tc4, s2=s2, sD=sD: e.matmul(
                                pqk[:, pD_base + sD * 512 : pD_base + sD * 512 + 128],
                                kd2t[s2][:, :],
                                vgt[:, tc4 * D + h * 128 : tc4 * D + (h + 1) * 128],
                                start=(tc4 == 0), stop=(tc4 == 3)), waits=w, inc=cT, n=1)
                            t_dmm = TK(cT)
                            kd2_free[s2] = t_dmm
                        E("vector", lambda e, h=h, sD=h % 4: e.scalar_tensor_tensor(
                            Dsb[:, h * 128 : (h + 1) * 128],
                            pqk[:, pD_base + sD * 512 : pD_base + sD * 512 + 128],
                            0.0, Dsb[:, h * 128 : (h + 1) * 128], Alu.bypass, Alu.add),
                          waits=[t_dmm, t_Dadd], inc=cV, n=1)
                        t_Dadd = TK(cV)
                        pD_free[h % 4] = t_Dadd
                    vgf_free = t_Dadd
                    for tc4 in range(4):
                        tcg = g * 4 + tc4
                        t_spv[tcg] = E("gpsimd", lambda e, tcg=tcg, tc4=tc4: e.dma_start(
                            out=vsp_d[tcg, :, :], in_=vgt[:, tc4 * D : (tc4 + 1) * D]),
                          waits=[t_Dadd], inc=GL.nxt(), n=16)
                        vgt_free[tc4] = t_spv[tcg]
                    kg_free = kspl

                t_dst = E("gpsimd", lambda e: e.dma_start(
                    out=dsh_d[:, :, :].rearrange("h p f -> p h f"),
                    in_=Dsb[:, :].rearrange("p (h f) -> p h f", h=H)),
                  waits=[t_Dadd], inc=GL.nxt(), n=16)
                if ccut >= 5:
                    E("gpsimd", lambda e: e.collective_compute(
                        "AllGather", Alu.bypass, replica_groups=[list(range(NCORE))],
                        ins=[dsh_d.ap().opt()], outs=[dfull_d.ap().opt()]),
                      waits=[t_dst], inc=cC, n=1)
                    t_dag = TK(cC)
                if debug >= 2 and ccut >= 5:
                    E("gpsimd", lambda e: e.dma_start(out=dbg["dloc"][:, :, :],
                                                      in_=dsh_d[:, :, :]),
                      waits=[t_dst], inc=cOg, n=16)
                    for nm, src in (("q", qsp_d), ("k", ksp_d), ("v", vsp_d)):
                        E("gpsimd", lambda e, nm=nm, src=src: e.dma_start(
                            out=dbg[nm][:, :, :], in_=src[:, :, :]), waits=GL.cur(),
                          inc=cOg, n=16)
                barC = bar()

            if stage.startswith("C"):
                E("sync", lambda e: e.dma_start(out=out_d[:, :], in_=x_in[:, :]),
                  waits=barC, inc=cO, n=16)
                finish([])
                return nc

            with ExitStack() as pe:
                esb = lambda n, sh, dt=F32: pe.enter_context(nc.sbuf_tensor(n, sh, dt))
                S = esb("eS", [128, H * 128])
                qf = [esb(f"eqf{i}", [128, D], BF16) for i in range(2)]
                kf = [esb(f"ekf{i}", [128, D], BF16) for i in range(2)]
                vf = [esb(f"evf{i}", [128, D], BF16) for i in range(2)]
                Og = esb("eOg", [128, H * 512], BF16)
                sTm = [esb(f"esTm{i}", [128, 128], BF16) for i in range(2)]
                qdb = [esb(f"eqdb{i}", [128, 128]) for i in range(2)]
                kdt = [esb(f"ekdt{i}", [128, 128], BF16) for i in range(2)]
                xe2 = [esb(f"exe{i}", [128, D]) for i in range(2)]
                x2b2 = [esb(f"ex2b{i}", [128, D]) for i in range(2)]
                eC_ = esb("eeC", [128, D])
                eEi = esb("eeEi", [128, D], I32)
                x2qtm = esb("ex2qtm", [128, D], BF16)
                wo = [esb(f"ewo{i}", [128, 1024], BF16) for i in range(2)]
                emx = esb("eemx", [128, 64])
                es_ = esb("ees", [128, 64])
                ers = esb("eers", [128, 64])
                elnm = esb("eelnm", [128, 1])
                elnr = esb("eelnr", [128, 1])
                elna = esb("eelna", [128, 1])
                elnb = esb("eelnb", [128, 1])
                psc = pe.enter_context(nc.psum_tensor("psc", [128, 512], F32))
                ptrE = pe.enter_context(nc.psum_tensor("ptrE", [128, 512], F32))
                po = pe.enter_context(nc.psum_tensor("po", [128, 512], F32))
                pa = pe.enter_context(nc.psum_tensor("pa", [128, 512], F32))
                pop = pe.enter_context(nc.psum_tensor("pop", [128, 4 * 512], F32))

                E("vector", lambda e: e.memset(S[:, :], 0.0), waits=barC, inc=cV, n=1)
                t_S = TK(cV)
                dj_use = [None, None]
                for j in range(NCORE):
                    buf = xe2[j % 2]
                    w = [t_dag] + barC + ([dj_use[j % 2]] if dj_use[j % 2] else [])
                    t_dj = E("sync", lambda e, j=j, buf=buf: e.dma_start(
                        out=buf[:, 0 : H * 128].rearrange("p (h f) -> p h f", h=H),
                        in_=dfull_d[j, :, :, :].rearrange("h p f -> p h f")),
                      waits=w, inc=QL.nxt(), n=16)
                    for h in range(H):
                        E("vector", lambda e, j=j, h=h, buf=buf: e.scalar_tensor_tensor(
                            S[:, h * 128 : (h + 1) * 128],
                            buf[:, h * 128 : (h + 1) * 128],
                            coefs[:, j * H + h : j * H + h + 1],
                            S[:, h * 128 : (h + 1) * 128], Alu.mult, Alu.add),
                          waits=[t_dj, t_S, t_coefs], inc=cV, n=1)
                        t_S = TK(cV)
                    dj_use[j % 2] = t_S
                if debug >= 2:
                    E("gpsimd", lambda e: e.dma_start(
                        out=dbg["sinit"][:, :, :].rearrange("h p f -> p h f"),
                        in_=S[:, :].rearrange("p (h f) -> p h f", h=H)),
                      waits=[t_S], inc=cOg, n=16)
                    t_S = TK(cOg)

                qkv_free = [None, None]
                slot_sc = [None] * 4
                slot_tr = [None] * 4
                slot_o = [None] * 4
                slot_a = [None] * 4
                sTm_free = [None, None]
                qdb_free = [None, None]
                kdt_free = [None, None]
                pop_free = [None] * 4
                wo_use = [None, None]
                Og_read = None
                xe_free = [[dj_use[0]], [dj_use[1]]]
                t_xe_d = [None, None]
                t_x2qtap = None
                t_elntap = None
                t_q5 = []
                t_q5p = None
                t_eCfree = None
                x2b_free = [None, None]
                t_Supd = t_S
                t_Supd_h = {}
                first_sc2 = barC

                for g in range(NG):
                    t_og = None
                    og_t = {}
                    for tc4 in range(4):
                        tcg = g * 4 + tc4
                        b = tcg % 2
                        w = list(t_spq.get(tcg) or []) + list(t_spk.get(tcg) or []) + \
                            ([t_spv[tcg]] if tcg in t_spv else []) + \
                            ([qkv_free[b]] if qkv_free[b] else [])
                        t_q1 = E("sync", lambda e, tcg=tcg, b=b: e.dma_start(
                            out=qf[b][:, :], in_=qsp_d[tcg, :, :]), waits=w,
                            inc=QL.nxt(), n=16)
                        t_q2 = E("sync", lambda e, tcg=tcg, b=b: e.dma_start(
                            out=kf[b][:, :], in_=ksp_d[tcg, :, :]), inc=QL.nxt(), n=16)
                        t_q3 = E("sync", lambda e, tcg=tcg, b=b: e.dma_start(
                            out=vf[b][:, :], in_=vsp_d[tcg, :, :]), inc=QL.nxt(), n=16)
                        t_qkv = [t_q1, t_q2, t_q3]
                        last_use = None
                        for h in range(H):
                            s4 = 0
                            s2 = h % 2
                            hsl = slice(h * 128, (h + 1) * 128)
                            w = t_qkv + ([slot_sc[s4]] if slot_sc[s4] else [])
                            E("tensor", lambda e, b=b, hsl=hsl, s4=s4: e.matmul(
                                psc[:, s4 * 128 : (s4 + 1) * 128], kf[b][:, hsl], qf[b][:, hsl],
                                start=True, stop=True), waits=w, inc=cT, n=1)
                            t_sc = TK(cT)
                            w = [t_sc, t_maskT] + ([sTm_free[s2]] if sTm_free[s2] else [])
                            E("vector", lambda e, h=h, s4=s4, s2=s2: e.tensor_tensor(
                                sTm[s2][:, :], psc[:, s4 * 128 : (s4 + 1) * 128],
                                maskT[:, h * 128 : (h + 1) * 128], Alu.mult),
                              waits=w, inc=cV, n=1)
                            t_stm = TK(cV)
                            slot_sc[s4] = t_stm
                            w = t_qkv + [t_qdec] + ([qdb_free[s2]] if qdb_free[s2] else [])
                            E("vector", lambda e, b=b, h=h, hsl=hsl, s2=s2: e.tensor_tensor(
                                qdb[s2][:, :], qf[b][:, hsl],
                                qdec[:, h * 128 : (h + 1) * 128], Alu.mult),
                              waits=w, inc=cV, n=1)
                            t_qdb = TK(cV)
                            w = t_qkv + [t_ident] + ([slot_tr[s4]] if slot_tr[s4] else [])
                            E("tensor", lambda e, b=b, hsl=hsl, s4=s4: e.transpose(
                                _psb(ptrE, s4 * 128, (s4 + 1) * 128), kf[b][:, hsl],
                                ident_b[:, :]), waits=w, inc=cT, n=1)
                            t_ktr = TK(cT)
                            w = [t_ktr, t_dec2] + ([kdt_free[s2]] if kdt_free[s2] else [])
                            E("vector", lambda e, h=h, s4=s4, s2=s2: e.tensor_scalar(
                                kdt[s2][:, :], _psb(ptrE, s4 * 128, (s4 + 1) * 128),
                                kdcol[:, h : h + 1], None, Alu.mult), waits=w, inc=cV, n=1)
                            t_kdt = TK(cV)
                            slot_tr[s4] = t_kdt
                            w = [t_stm] + t_qkv + ([slot_o[s4]] if slot_o[s4] else [])
                            E("tensor", lambda e, b=b, hsl=hsl, s4=s4, s2=s2: e.matmul(
                                po[:, s4 * 128 : (s4 + 1) * 128], vf[b][:, hsl], sTm[s2][:, :],
                                start=True, stop=False), waits=w, inc=cT, n=1)
                            E("tensor", lambda e, hsl=hsl, s4=s4, s2=s2: e.matmul(
                                po[:, s4 * 128 : (s4 + 1) * 128], S[:, hsl], qdb[s2][:, :],
                                start=False, stop=True),
                              waits=[t_qdb, t_Supd_h.get(h, t_Supd)], inc=cT, n=1)
                            t_omm = TK(cT)
                            sTm_free[s2] = t_omm
                            qdb_free[s2] = t_omm
                            w = [t_kdt] + t_qkv + ([slot_a[s4]] if slot_a[s4] else [])
                            E("tensor", lambda e, b=b, hsl=hsl, s4=s4, s2=s2: e.matmul(
                                pa[:, s4 * 128 : (s4 + 1) * 128], kdt[s2][:, :], vf[b][:, hsl],
                                start=True, stop=True), waits=w, inc=cT, n=1)
                            t_amm = TK(cT)
                            kdt_free[s2] = t_amm
                            w = [t_omm] + ([Og_read] if Og_read else [])
                            E("vector", lambda e, h=h, tc4=tc4, s4=s4: e.tensor_copy(
                                Og[:, h * 512 + tc4 * 128 : h * 512 + (tc4 + 1) * 128],
                                po[:, s4 * 128 : (s4 + 1) * 128]), waits=w, inc=cV, n=1)
                            t_oev = TK(cV)
                            slot_o[s4] = t_oev
                            t_og = t_oev
                            og_t[(h, tc4)] = t_oev
                            E("vector", lambda e, hsl=hsl, h=h, s4=s4: e.scalar_tensor_tensor(
                                S[:, hsl], S[:, hsl], bdcol[:, h : h + 1],
                                pa[:, s4 * 128 : (s4 + 1) * 128], Alu.mult, Alu.add),
                              waits=[t_amm, t_omm], inc=cV, n=1)
                            t_Supd = TK(cV)
                            t_Supd_h[h] = t_Supd
                            slot_a[s4] = t_Supd
                            last_use = t_Supd
                        qkv_free[b] = last_use

                    for tcp in range(2):
                        x2_done = [None, None]
                        for och in range(2):
                            def opinit(e, och=och):
                                r = None
                                for tc2 in range(2):
                                    for oc2 in range(2):
                                        oc = och * 2 + oc2
                                        r = e.matmul(
                                            pop[:, (tc2 * 2 + oc2) * 512 : (tc2 * 2 + oc2 + 1) * 512],
                                            ones_row[0:1, 0:128],
                                            bout_row[0:1, oc * 512 : (oc + 1) * 512],
                                            start=True, stop=False)
                                return r
                            w = [t_bout, t_ones] + [t for t in pop_free if t]
                            E("tensor", opinit, waits=w, inc=cT, n=1)
                            for kc in range(16):
                                t_wo = E("sync", lambda e, kc=kc, och=och: e.dma_start(
                                    out=wo[kc % 2][:, :], in_=wout_bf(kc, och * 1024, 1024)),
                                  waits=[t_packagQ] + ([wo_use[kc % 2]] if wo_use[kc % 2] else []),
                                  inc=QL.nxt(), n=16)

                                def opmm(e, kc=kc, tcp=tcp):
                                    r = None
                                    for tc2 in range(2):
                                        tl = (tcp * 2 + tc2) * 128
                                        for oc2 in range(2):
                                            r = e.matmul(
                                                pop[:, (tc2 * 2 + oc2) * 512 : (tc2 * 2 + oc2 + 1) * 512],
                                                Og[:, kc * 512 + tl : kc * 512 + tl + 128],
                                                wo[kc % 2][:, oc2 * 512 : (oc2 + 1) * 512],
                                                start=False, stop=(kc == 15))
                                    return r
                                E("tensor", opmm,
                                  waits=[t_wo, og_t[(kc, tcp * 2)],
                                         og_t[(kc, tcp * 2 + 1)]], inc=cT, n=1)
                                wo_use[kc % 2] = TK(cT)
                            t_ops = TK(cT)
                            Og_read = t_ops
                            for tc2 in range(2):
                                tcg = g * 4 + tcp * 2 + tc2
                                if och == 0:
                                    w = list(xe_free[tc2] or [])
                                    t_xe_d[tc2] = E("sync", lambda e, tcg=tcg, tc2=tc2:
                                      e.dma_start(
                                        out=xe2[tc2][:, :],
                                        in_=x_in[tcg * 128 : (tcg + 1) * 128, :]),
                                      waits=w, inc=QL.nxt(), n=16)
                                    xe_free[tc2] = None
                                t_xe = t_xe_d[tc2]
                                for oc2 in range(2):
                                    oc = och * 2 + oc2
                                    bk = slice((tc2 * 2 + oc2) * 512, (tc2 * 2 + oc2 + 1) * 512)
                                    osl = slice(oc * 512, (oc + 1) * 512)
                                    w = [t_ops, t_garep, t_xe]
                                    if och == 0 and x2b_free[tc2]:
                                        w.extend(x2b_free[tc2])
                                    E("vector", lambda e, bk=bk, osl=osl, tc2=tc2: e.tensor_tensor(
                                        eC_[:, 0:512], pop[:, bk], ga_rep[:, osl], Alu.mult),
                                      waits=w, inc=cV, n=1)
                                    E("vector", lambda e, bk=bk, osl=osl, tc2=tc2: e.tensor_tensor(
                                        x2b2[tc2][:, osl], eC_[:, 0:512], xe2[tc2][:, osl],
                                        Alu.add), inc=cV, n=1)
                                    pop_free[tc2 * 2 + oc2] = TK(cV)
                            x2_done[0 if och == 0 else 0] = TK(cV)
                        t_x2v = TK(cV)
                        for tc2 in range(2):
                            tcg = g * 4 + tcp * 2 + tc2
                            xe_free[tc2] = t_x2v
                            t_x2sp[tcg] = E("gpsimd", lambda e, tcg=tcg, tc2=tc2: e.dma_start(
                                out=x2sp_d[tcg * 128 : (tcg + 1) * 128, :],
                                in_=x2b2[tc2][:, :]), waits=[t_x2v], inc=GL.nxt(), n=16)
                            t_x2tap = None
                            if debug >= 3:
                                t_x2tap = E("gpsimd", lambda e, tcg=tcg, tc2=tc2: e.dma_start(
                                    out=dbg["x2"][tcg * 128 : (tcg + 1) * 128, :],
                                    in_=x2b2[tc2][:, :]), inc=cOg, n=16)
                            xb = x2b2[tc2]
                            E("vector", lambda e, xb=xb: e.reduce_sum(elna[:, :], xb[:, :], AX),
                              waits=[t_x2v], inc=cV, n=1)
                            t_sx = TK(cV)
                            E("scalar", lambda e, xb=xb: e.activation(
                                eC_[:, :], xb[:, :], Act.Square, accum_out=elnb[:, :]),
                              waits=[t_x2v] + (first_sc2 or []) +
                                    ([t_eCfree] if t_eCfree else []) + (t_q5p or []),
                              inc=cS, n=1)
                            first_sc2 = None
                            t_sq2 = TK(cS)
                            E("vector", lambda e: e.tensor_scalar(elnm[:, :], elna[:, :], 1.0 / D,
                                                                  None, Alu.mult),
                              waits=[t_sx, TK(cV)] + (t_elntap or []), inc=cV, n=1)
                            E("vector", lambda e: e.tensor_scalar(elnb[:, :], elnb[:, :], 1.0 / D,
                                                                  None, Alu.mult),
                              waits=[t_sq2, TK(cV)], inc=cV, n=1)
                            E("vector", lambda e: e.scalar_tensor_tensor(
                                elna[:, :], elnm[:, :], elnm[:, 0:1], elnb[:, :],
                                Alu.mult, Alu.subtract), waits=[TK(cV)], inc=cV, n=1)
                            E("vector", lambda e: e.tensor_scalar(elna[:, :], elna[:, :], -1.0,
                                                                  None, Alu.mult),
                              waits=[TK(cV)], inc=cV, n=1)
                            t_var = TK(cV)
                            E("scalar", lambda e: e.activation(elna[:, :], elna[:, :], Act.Sqrt,
                                                               bias=cp[:, CEPS : CEPS + 1]),
                              waits=[t_var], inc=cS, n=1)
                            t_std = TK(cS)
                            E("vector", lambda e: e.reciprocal(elnr[:, :], elna[:, :]),
                              waits=[t_std], inc=cV, n=1)
                            eA = xe2[tc2]
                            eB = xe2[1 - tc2]
                            E("vector", lambda e, xb=xb, eA=eA: e.tensor_scalar(
                                eA[:, :], xb[:, :], elnm[:, 0:1], elnr[:, 0:1],
                                Alu.subtract, Alu.mult),
                              waits=list(xe_free[1 - tc2] or []) + [TK(cV)] + (t_q5p or []),
                              inc=cV, n=1)
                            E("vector", lambda e, eA=eA: e.tensor_tensor(
                                eA[:, :], eA[:, :], cb1_rep[:, :], Alu.mult),
                              waits=[t_mods], inc=cV, n=1)
                            E("vector", lambda e, eA=eA: e.tensor_tensor(
                                eA[:, :], eA[:, :], sb_rep[:, :], Alu.add), inc=cV, n=1)
                            t_h2 = TK(cV)
                            t_h2tap = None
                            if debug >= 3:
                                t_h2tap = E("gpsimd", lambda e, tcg=tcg, eA=eA: e.dma_start(
                                    out=dbg["h2"][tcg * 128 : (tcg + 1) * 128, :],
                                    in_=eA[:, :]), waits=[t_h2], inc=cOg, n=16)
                                t1e = E("gpsimd", lambda e, tcg=tcg: e.dma_start(
                                    out=dbg["eln"][tcg, :, 0:1], in_=elnm[:, :],
                                    allow_slow_non_contiguous=True),
                                  waits=[t_h2], inc=cOg, n=16)
                                t2e = E("gpsimd", lambda e, tcg=tcg: e.dma_start(
                                    out=dbg["eln"][tcg, :, 1:2], in_=elnr[:, :],
                                    allow_slow_non_contiguous=True),
                                  waits=[t_h2], inc=cOg, n=16)
                                t_elntap = [t1e, t2e]
                            E("vector", lambda e, eA=eA: e.tensor_reduce(
                                emx[:, :].rearrange("p (b e) -> p b e", e=1), _blkv(eA, D), AX,
                                Alu.max, apply_absolute_value=True), inc=cV, n=1)
                            E("vector", lambda e: e.tensor_scalar(es_[:, :], emx[:, :], INV448,
                                                                  1e-30, Alu.mult, Alu.max),
                              waits=[TK(cV)] + (t_q5p or []), inc=cV, n=1)
                            E("vector", lambda e: e.reciprocal(ers[:, :], es_[:, :]),
                              waits=[TK(cV)], inc=cV, n=1)
                            E("vector", lambda e, eA=eA, eB=eB: e.tensor_tensor(
                                _blkv(eB, D), _blkv(eA, D), _bcv(ers, D), Alu.mult),
                              waits=[TK(cV)], inc=cV, n=1)
                            t_y = TK(cV)
                            if debug >= 5:
                                t_q5.append(E("gpsimd", lambda e, tcg=tcg, eB=eB: e.dma_start(
                                    out=dbg["ey"][tcg * 128 : (tcg + 1) * 128, :], in_=eB[:, :]),
                                  waits=[t_y], inc=cOg, n=16))
                            E("scalar", lambda e, eB=eB: e.activation(eC_[:, :], eB[:, :],
                                                                      Act.Square),
                              waits=[t_y] + (t_q5p or []), inc=cS, n=1)
                            t_sq3 = TK(cS)
                            E("vector", lambda e: e.tensor_scalar(eC_[:, :], eC_[:, :],
                                                                  0.000244140625, None, Alu.max),
                              waits=[t_sq3], inc=cV, n=1)
                            t_cl = TK(cV)
                            E("scalar", lambda e: e.activation(eC_[:, :], eC_[:, :], Act.Ln),
                              waits=[t_cl], inc=cS, n=1)
                            t_ln = TK(cS)
                            E("vector", lambda e: e.tensor_scalar(eEi[:, :], eC_[:, :], INV_2LN2,
                                                                  -0.5, Alu.mult, Alu.add),
                              waits=[t_ln] + (t_q5p or []), inc=cV, n=1)
                            t_ei = TK(cV)
                            t_eetap = None
                            if debug >= 5:
                                t_eetap = E("gpsimd", lambda e, tcg=tcg: e.dma_start(
                                    out=dbg["ee"][tcg * 128 : (tcg + 1) * 128, :], in_=eEi[:, :]),
                                  waits=[t_ei], inc=cOg, n=16)
                                t_q5.append(t_eetap)
                            E("scalar", lambda e: e.activation(eC_[:, :], eEi[:, :], Act.Exp,
                                                               bias=cp[:, CNB3 : CNB3 + 1],
                                                               scale=cp[:, CLN2 : CLN2 + 1]),
                              waits=[t_ei], inc=cS, n=1)
                            E("scalar", lambda e, eA=eA: e.activation(eA[:, :], eEi[:, :], Act.Exp,
                                                                      bias=cp[:, CPB3 : CPB3 + 1],
                                                                      scale=cp[:, CNLN2 : CNLN2 + 1]),
                              waits=([t_h2tap] if t_h2tap else []), inc=cS, n=1)
                            t_exp = TK(cS)
                            if debug >= 5:
                                t_q5.append(E("gpsimd", lambda e, tcg=tcg: e.dma_start(
                                    out=dbg["eu"][tcg * 128 : (tcg + 1) * 128, :], in_=eC_[:, :]),
                                  waits=[t_exp], inc=cOg, n=16))
                            E("vector", lambda e, eA=eA, eB=eB: e.tensor_tensor(
                                eA[:, :], eB[:, :], eA[:, :], Alu.mult), waits=[t_exp],
                              inc=cV, n=1)
                            t_z5 = TK(cV)
                            t_eztap = None
                            if debug >= 5:
                                t_eztap = E("gpsimd", lambda e, tcg=tcg, eA=eA: e.dma_start(
                                    out=dbg["ez"][tcg * 128 : (tcg + 1) * 128, :], in_=eA[:, :]),
                                  waits=[t_z5], inc=cOg, n=16)
                                t_q5.append(t_eztap)
                            E("vector", lambda e, eA=eA: e.tensor_copy(eEi[:, :], eA[:, :]),
                              waits=([t_eetap] if t_eetap else []), inc=cV, n=1)
                            t_r5 = TK(cV)
                            if debug >= 5:
                                t_q5.append(E("gpsimd", lambda e, tcg=tcg: e.dma_start(
                                    out=dbg["er"][tcg * 128 : (tcg + 1) * 128, :], in_=eEi[:, :]),
                                  waits=[t_r5], inc=cOg, n=16))
                            E("vector", lambda e, eA=eA: e.tensor_tensor(
                                eA[:, :], eEi[:, :], eC_[:, :], Alu.mult),
                              waits=([t_eztap] if t_eztap else []), inc=cV, n=1)
                            t_eCfree = TK(cV)
                            E("vector", lambda e, eA=eA: e.tensor_tensor(
                                _blkv(x2qtm, D), _blkv(eA, D), _bcv(es_, D), Alu.mult),
                              waits=([t_x2qtap] if t_x2qtap else []), inc=cV, n=1)
                            t_dqx = TK(cV)
                            if debug >= 5:
                                t_q5.append(E("gpsimd", lambda e, tcg=tcg: e.dma_start(
                                    out=dbg["es"][tcg, :, :], in_=es_[:, :]),
                                  waits=[t_dqx], inc=cOg, n=16))
                            xe_free[tc2] = [t_dqx] + list(t_q5)
                            xe_free[1 - tc2] = [t_dqx] + list(t_q5)
                            t_q5p = list(t_q5)
                            t_q5 = []
                            x2b_free[tc2] = [t_dqx, t_x2sp[tcg]]
                            if debug >= 3:
                                t_x2qtap = E("gpsimd", lambda e, tcg=tcg: e.dma_start(
                                    out=dbg["x2q"][0:16, :, tcg * 128 : (tcg + 1) * 128]
                                    .rearrange("a p t -> p a t"),
                                    in_=x2qtm[:, :].rearrange("p (a t) -> p a t", a=16)),
                                  waits=[t_dqx], inc=cOg, n=16)
                            for m in range(16):
                                s4 = 0
                                w = [t_dqx, t_ident] + ([slot_tr[s4]] if slot_tr[s4] else [])
                                E("tensor", lambda e, m=m, s4=s4: e.transpose(
                                    _psb(ptrE, s4 * 128, (s4 + 1) * 128),
                                    x2qtm[:, m * 128 : (m + 1) * 128], ident_b[:, :]),
                                  waits=w, inc=cT, n=1)
                                t_tr = TK(cT)
                                E("vector", lambda e, m=m, s4=s4, tcg=tcg: e.tensor_copy(
                                    x2q_sl(m, tcg * 128, (tcg + 1) * 128),
                                    _psb(ptrE, s4 * 128, (s4 + 1) * 128)),
                                  waits=[t_tr], inc=cV, n=1)
                                slot_tr[s4] = TK(cV)
                            x2b_free[tc2] = [TK(cV), t_x2sp[tcg]] + \
                                ([t_x2tap] if t_x2tap else [])
                t_x2q_last = TK(cV)
                barE = bar()

            if stage == "E":
                E("sync", lambda e: e.dma_start(out=out_d[:, :], in_=x2sp_d[:, :]),
                  waits=barE, inc=cO, n=16)
                finish([])
                return nc

            with ExitStack() as pg:
                gsb = lambda n, sh, dt=F32: pg.enter_context(nc.sbuf_tensor(n, sh, dt))
                gT = gsb("ggT", [128, 16 * 512], BF16)
                uT = gsb("guT", [128, 16 * 512], BF16)
                gs = gsb("ggs", [128, D])
                hb = gsb("ghb", [128, D])
                htm = gsb("ghtm", [128, D])
                htq = gsb("ghtq", [128, D], BF16)
                hqf = gsb("ghqf", [128, D], BF16)
                gB = gsb("ggB", [128, D])
                gC = gsb("ggC", [128, D])
                gEi = gsb("gEi", [128, D], I32)
                gmx = gsb("ggmx", [128, 64])
                gss = gsb("ggss", [128, 64])
                grs = gsb("ggrs", [128, 64])
                pgg = pg.enter_context(nc.psum_tensor("pgg", [128, 4 * 512], F32))
                pgu = pg.enter_context(nc.psum_tensor("pgu", [128, 4 * 512], F32))

                gu_use = None
                ptr_free = [None] * 4
                hqf_free = None
                first = barE

                for m4 in range(16):
                    w = list(first or []) + ([gu_use] if gu_use else []) + [t_packag]
                    first = None
                    t_w = []
                    for kc in range(16):
                        rnk = (m4 * 512) // GSH
                        roff = (m4 * 512) % GSH
                        t_w.append(E("sync", lambda e, kc=kc, rnk=rnk, roff=roff: e.dma_start(
                            out=gT[:, kc * 512 : (kc + 1) * 512],
                            in_=pk2d(rnk, OFF_G + roff * D, 512, D)[:, kc * 128 : (kc + 1) * 128],
                            transpose=True), waits=w, inc=QL.nxt(), n=16))
                        w = []
                        t_w.append(E("sync", lambda e, kc=kc, rnk=rnk, roff=roff: e.dma_start(
                            out=uT[:, kc * 512 : (kc + 1) * 512],
                            in_=pk2d(rnk, OFF_U + roff * D, 512, D)[:, kc * 128 : (kc + 1) * 128],
                            transpose=True), inc=QL.nxt(), n=16))
                    for mi in range(4):
                        m = m4 * 4 + mi
                        msl = slice(mi * 128, (mi + 1) * 128)

                        def gumm(e, msl=msl):
                            r = None
                            for kc in range(16):
                                for ts in range(4):
                                    r = e.matmul(pgg[:, ts * 512 : (ts + 1) * 512],
                                                 gT[:, kc * 512 : (kc + 1) * 512][:, msl],
                                                 x2q_sl(kc, ts * 512, (ts + 1) * 512),
                                                 start=(kc == 0), stop=(kc == 15))
                                for ts in range(4):
                                    r = e.matmul(pgu[:, ts * 512 : (ts + 1) * 512],
                                                 uT[:, kc * 512 : (kc + 1) * 512][:, msl],
                                                 x2q_sl(kc, ts * 512, (ts + 1) * 512),
                                                 start=(kc == 0), stop=(kc == 15))
                            return r
                        w2 = t_w + [t_x2q_last] + [t for t in ptr_free if t]
                        E("tensor", gumm, waits=w2, inc=cT, n=1)
                        t_gu = TK(cT)
                        E("scalar", lambda e: e.activation(gs[:, :], pgg[:, :], Act.Silu),
                          waits=[t_gu], inc=cS, n=1)
                        t_silu = TK(cS)
                        E("vector", lambda e: e.tensor_tensor(hb[:, :], gs[:, :], pgu[:, :],
                                                              Alu.mult), waits=[t_silu],
                          inc=cV, n=1)
                        t_h = TK(cV)
                        for t16 in range(16):
                            s4 = t16 % 4
                            w3 = [t_h, t_ident] + ([ptr_free[s4]] if ptr_free[s4] else [])
                            E("tensor", lambda e, t16=t16, s4=s4: e.transpose(
                                pgg[:, s4 * 512 : s4 * 512 + 128],
                                hb[:, t16 * 128 : (t16 + 1) * 128], ident_f[:, :]),
                              waits=w3, inc=cT, n=1)
                            t_tr = TK(cT)
                            E("vector", lambda e, t16=t16, s4=s4: e.tensor_copy(
                                htm[:, t16 * 128 : (t16 + 1) * 128],
                                pgg[:, s4 * 512 : s4 * 512 + 128]), waits=[t_tr], inc=cV, n=1)
                            ptr_free[s4] = TK(cV)
                        t_htm = TK(cV)
                        E("vector", lambda e: e.tensor_reduce(
                            gmx[:, :].rearrange("p (b e) -> p b e", e=1), _blkv(htm, D), AX,
                            Alu.max, apply_absolute_value=True), waits=[t_htm], inc=cV, n=1)
                        E("vector", lambda e: e.tensor_scalar(gss[:, :], gmx[:, :], INV448, 1e-30,
                                                              Alu.mult, Alu.max),
                          waits=[TK(cV)], inc=cV, n=1)
                        E("vector", lambda e: e.reciprocal(grs[:, :], gss[:, :]),
                          waits=[TK(cV)], inc=cV, n=1)
                        E("vector", lambda e: e.tensor_tensor(_blkv(gB, D), _blkv(htm, D),
                                                              _bcv(grs, D), Alu.mult),
                          waits=[TK(cV)], inc=cV, n=1)
                        t_y = TK(cV)
                        E("scalar", lambda e: e.activation(gC[:, :], gB[:, :], Act.Square),
                          waits=[t_y], inc=cS, n=1)
                        t_sq = TK(cS)
                        E("vector", lambda e: e.tensor_scalar(gC[:, :], gC[:, :], 0.000244140625,
                                                              None, Alu.max), waits=[t_sq],
                          inc=cV, n=1)
                        t_cl = TK(cV)
                        E("scalar", lambda e: e.activation(gC[:, :], gC[:, :], Act.Ln),
                          waits=[t_cl], inc=cS, n=1)
                        t_ln2 = TK(cS)
                        E("vector", lambda e: e.tensor_scalar(gEi[:, :], gC[:, :], INV_2LN2, -0.5,
                                                              Alu.mult, Alu.add),
                          waits=[t_ln2], inc=cV, n=1)
                        t_ei = TK(cV)
                        E("scalar", lambda e: e.activation(gC[:, :], gEi[:, :], Act.Exp,
                                                           bias=cp[:, CNB3 : CNB3 + 1],
                                                           scale=cp[:, CLN2 : CLN2 + 1]),
                          waits=[t_ei], inc=cS, n=1)
                        E("scalar", lambda e: e.activation(htm[:, :], gEi[:, :], Act.Exp,
                                                           bias=cp[:, CPB3 : CPB3 + 1],
                                                           scale=cp[:, CNLN2 : CNLN2 + 1]),
                          inc=cS, n=1)
                        t_exp = TK(cS)
                        E("vector", lambda e: e.tensor_tensor(htm[:, :], gB[:, :], htm[:, :],
                                                              Alu.mult), waits=[t_exp],
                          inc=cV, n=1)
                        E("vector", lambda e: e.tensor_copy(gEi[:, :], htm[:, :]), inc=cV, n=1)
                        E("vector", lambda e: e.tensor_tensor(htm[:, :], gEi[:, :], gC[:, :],
                                                              Alu.mult), inc=cV, n=1)
                        E("vector", lambda e: e.tensor_tensor(_blkv(htq, D), _blkv(htm, D),
                                                              _bcv(gss, D), Alu.mult),
                          inc=cV, n=1)
                        t_htq = TK(cV)
                        for t16 in range(16):
                            s4 = t16 % 4
                            w3 = [t_htq, t_ident] + ([ptr_free[s4]] if ptr_free[s4] else [])
                            E("tensor", lambda e, t16=t16, s4=s4: e.transpose(
                                _psb(pgu, s4 * 512, s4 * 512 + 128),
                                htq[:, t16 * 128 : (t16 + 1) * 128], ident_b[:, :]),
                              waits=w3, inc=cT, n=1)
                            t_tr = TK(cT)
                            w4 = [t_tr] + ([hqf_free] if hqf_free else [])
                            E("vector", lambda e, t16=t16, s4=s4: e.tensor_copy(
                                hqf[:, t16 * 128 : (t16 + 1) * 128],
                                _psb(pgu, s4 * 512, s4 * 512 + 128)), waits=w4, inc=cV, n=1)
                            ptr_free[s4] = TK(cV)
                        t_hqf = TK(cV)
                        hqf_free = E("gpsimd", lambda e, m=m: e.dma_start(out=hq_d[m, :, :],
                                                                          in_=hqf[:, :]),
                                     waits=[t_hqf], inc=GL.nxt(), n=16)
                        gu_use = t_gu if mi < 3 else TK(cT)
                    gu_use = TK(cT)
                t_hq_all = GL.cur()
                if debug >= 4:
                    E("gpsimd", lambda e: e.dma_start(out=dbg["hq"][:, :, :], in_=hq_d[:, :, :]),
                      waits=t_hq_all, inc=cOg, n=16)
                barG1 = bar()

        with ExitStack() as pgd:
            dsb = lambda n, sh, dt=F32: pgd.enter_context(nc.sbuf_tensor(n, sh, dt))
            outst = dsb("dout", [128, 8 * D])
            fmt = dsb("dfmt", [128, 8 * 512], BF16)
            hqr = [dsb(f"dhqr{i}", [128, 1024], BF16) for i in range(2)]
            ddT = [dsb(f"dddT{i}", [128, 512], BF16) for i in range(2)]
            x2r = dsb("dx2r", [128, D])
            pdn = pgd.enter_context(nc.psum_tensor("pdn", [128, 8 * 512], F32))

            hq_use = [None, None]
            dd_use = [None, None]
            fmt_free = None
            outst_free = [None] * 8
            pdnrd = [None] * 4
            x2r_free = None
            first = barG1
            for tsp in range(2):
                out_wr = []
                for oq in range(4):
                    t_mm = None
                    for m in range(64):
                        b = m % 2
                        w = list(first or []) + [t_packag] + \
                            ([dd_use[b]] if dd_use[b] else [])
                        first = None
                        t_d1 = E("sync", lambda e, m=m, oq=oq, b=b: e.dma_start(
                            out=ddT[b][:, 0:256],
                            in_=pk2d(2 * oq, OFF_DW, DSH, IF)[:, m * 128 : (m + 1) * 128],
                            transpose=True), waits=w, inc=QL.nxt(), n=16)
                        t_d2 = E("sync", lambda e, m=m, oq=oq, b=b: e.dma_start(
                            out=ddT[b][:, 256:512],
                            in_=pk2d(2 * oq + 1, OFF_DW, DSH, IF)[:, m * 128 : (m + 1) * 128],
                            transpose=True), inc=QL.nxt(), n=16)
                        w = list(t_hq_all) + ([hq_use[b]] if hq_use[b] else [])
                        t_d3 = E("sync", lambda e, m=m, tsp=tsp, b=b: e.dma_start(
                            out=hqr[b][:, :], in_=hq_d[m, :, tsp * 1024 : (tsp + 1) * 1024]),
                          waits=w, inc=QL.nxt(), n=16)
                        t_ld = [t_d1, t_d2, t_d3]

                        def dnmm(e, m=m, b=b):
                            r = None
                            for oc in range(4):
                                for ts2 in range(2):
                                    r = e.matmul(
                                        pdn[:, (oc * 2 + ts2) * 512 : (oc * 2 + ts2 + 1) * 512],
                                        ddT[b][:, oc * 128 : (oc + 1) * 128],
                                        hqr[b][:, ts2 * 512 : (ts2 + 1) * 512],
                                        start=(m == 0), stop=(m == 63))
                            return r
                        w = list(t_ld)
                        if m == 0 and fmt_free:
                            w.append(fmt_free)
                        E("tensor", dnmm, waits=w, inc=cT, n=1)
                        t_mm = TK(cT)
                        hq_use[b] = t_mm
                        dd_use[b] = t_mm
                    ev = []
                    for oc in range(4):
                        for ts2 in range(2):
                            bk = slice((oc * 2 + ts2) * 512, (oc * 2 + ts2 + 1) * 512)
                            E("vector", lambda e, bk=bk, oq=oq, oc=oc: e.tensor_scalar(
                                fmt[:, bk], pdn[:, bk], gbcol[:, oq * 4 + oc : oq * 4 + oc + 1],
                                None, Alu.mult), waits=[t_mm, t_geff], inc=cV, n=1)
                            ev.append(TK(cV))
                    t_ev = TK(cV)
                    fmt_free = None
                    for oc in range(4):
                        for ts2 in range(2):
                            for tt in range(4):
                                tch = tsp * 8 + ts2 * 4 + tt
                                s4 = tt
                                w = [t_ev, t_ident] + \
                                    ([pdnrd[s4]] if pdnrd[s4] else [])
                                E("tensor", lambda e, oc=oc, ts2=ts2, tt=tt, s4=s4: e.transpose(
                                    _psb(pdn, s4 * 512, s4 * 512 + 128),
                                    fmt[:, (oc * 2 + ts2) * 512 + tt * 128 :
                                        (oc * 2 + ts2) * 512 + (tt + 1) * 128],
                                    ident_b[:, :]), waits=w, inc=cT, n=1)
                                t_tr = TK(cT)
                                oco = oq * 4 + oc
                                w = [t_tr]
                                if outst_free[ts2 * 4 + tt]:
                                    w.append(outst_free[ts2 * 4 + tt])
                                E("vector", lambda e, tch=tch, oco=oco, s4=s4, ts2=ts2, tt=tt:
                                  e.tensor_copy(
                                    outst[:, (ts2 * 4 + tt) * D + oco * 128 :
                                          (ts2 * 4 + tt) * D + (oco + 1) * 128],
                                    _psb(pdn, s4 * 512, s4 * 512 + 128)), waits=w, inc=cV, n=1)
                                pdnrd[s4] = TK(cV)
                    fmt_free = TK(cV)
                t_alltr = TK(cV)
                for tl in range(8):
                    tcg = tsp * 8 + tl
                    w = [t_x2sp.get(tcg), t_alltr] + ([x2r_free] if x2r_free else [])
                    t_xr = E("sync", lambda e, tcg=tcg: e.dma_start(
                        out=x2r[:, :], in_=x2sp_d[tcg * 128 : (tcg + 1) * 128, :]),
                      waits=[t for t in w if t], inc=QL.nxt(), n=16)
                    E("vector", lambda e, tl=tl: e.tensor_tensor(
                        outst[:, tl * D : (tl + 1) * D], outst[:, tl * D : (tl + 1) * D],
                        x2r[:, :], Alu.add), waits=[t_xr], inc=cV, n=1)
                    t_add = TK(cV)
                    x2r_free = t_add
                    out_wr.append(E("sync", lambda e, tcg=tcg, tl=tl: e.dma_start(
                        out=out_d[tcg * 128 : (tcg + 1) * 128, :],
                        in_=outst[:, tl * D : (tl + 1) * D]), waits=[t_add],
                        inc=QL.nxt(), n=16))
                    outst_free[tl] = out_wr[-1]
                for i in range(8):
                    outst_free[i] = out_wr[i] if tsp == 0 else None

        finish([])
        return nc


_NC_CACHE = {}


def kernel(L0_attn_x, L0_attn_Wqkv, L0_attn_bqkv, L0_attn_Wout, L0_attn_bout,
           L0_attn_slopes, L0_ffn_gate_w, L0_ffn_up_w, L0_ffn_down_w, L0_c, L0_wmod,
           _stage="full", _debug=0):
    x = np.ascontiguousarray(np.asarray(L0_attn_x, dtype=np.float32).reshape(NCORE * T, D))
    wqkv = np.ascontiguousarray(np.asarray(L0_attn_Wqkv, dtype=np.float32))
    bqkv = np.ascontiguousarray(np.asarray(L0_attn_bqkv, dtype=np.float32).reshape(1, 3 * D))
    wout = np.ascontiguousarray(np.asarray(L0_attn_Wout, dtype=np.float32))
    bout = np.ascontiguousarray(np.asarray(L0_attn_bout, dtype=np.float32).reshape(1, D))
    slopes = np.ascontiguousarray(np.asarray(L0_attn_slopes, dtype=np.float32).reshape(1, H))
    gw = np.ascontiguousarray(np.asarray(L0_ffn_gate_w, dtype=np.float32))
    uw = np.ascontiguousarray(np.asarray(L0_ffn_up_w, dtype=np.float32))
    dw = np.ascontiguousarray(np.asarray(L0_ffn_down_w, dtype=np.float32))
    c = np.ascontiguousarray(np.asarray(L0_c, dtype=np.float32).reshape(1, D))
    wmod = np.ascontiguousarray(np.asarray(L0_wmod, dtype=np.float32))

    in_maps = []
    for i in range(NCORE):
        steps = np.zeros((1, NCORE), dtype=np.float32)
        maskr = np.zeros((1, NCORE), dtype=np.float32)
        for j in range(NCORE):
            if j < i:
                steps[0, j] = float(i - 1 - j)
                maskr[0, j] = 1.0
        in_maps.append({
            "x": x[i * T : (i + 1) * T, :],
            "wqkv_sh": wqkv[i * QSH : (i + 1) * QSH, :],
            "bqkv": bqkv,
            "wout_sh": wout[i * WOSH : (i + 1) * WOSH, :],
            "bout": bout,
            "slopes": slopes,
            "g_sh": gw[i * GSH : (i + 1) * GSH, :],
            "u_sh": uw[i * GSH : (i + 1) * GSH, :],
            "dw_sh": dw[i * DSH : (i + 1) * DSH, :],
            "c_in": c,
            "wmod_sh": np.ascontiguousarray(wmod[:, i * WMSH : (i + 1) * WMSH]),
            "steps": steps,
            "maskr": maskr,
        })

    key = (_stage, _debug)
    if key not in _NC_CACHE:
        _NC_CACHE[key] = build(stage=_stage, debug=_debug)
    nc = _NC_CACHE[key]
    res = run_bass_kernel_spmd(nc, in_maps, core_ids=list(range(NCORE)))
    out = np.concatenate([np.asarray(res.results[i]["out"]) for i in range(NCORE)], axis=0)
    out = out.reshape(1, NCORE * T, D)
    if _debug:
        return out, res
    return out


# version 06
# speedup vs baseline: 1.0038x; 1.0038x over previous
from contextlib import ExitStack

import numpy as np
import concourse.bass as bass
import concourse.mybir as mybir
from concourse.bass_utils import run_bass_kernel_spmd

F32 = mybir.dt.float32
BF16 = mybir.dt.bfloat16
I32 = mybir.dt.int32
Alu = mybir.AluOpType
Act = mybir.ActivationFunctionType
AX = mybir.AxisListType.X

NCORE = 8
T = 2048
D = 2048
H = 16
HD = 128
NTC = 16
NG = 4
IF = 8192
LN2 = float(np.float32(np.log(2.0)))
INV_2LN2 = 0.7213475204444817
INV448 = 1.0 / 448.0

GSH = IF // NCORE
DSH = D // NCORE
QSH = D // NCORE
WOSH = D // NCORE
WMSH = 12288 // NCORE

N_G = GSH * D
N_U = GSH * D
N_DW = DSH * IF
N_QKV = QSH * 3 * D
N_WO = WOSH * D
OFF_G = 0
OFF_U = N_G
OFF_DW = N_G + N_U
OFF_QKV = 0
OFF_WO = N_QKV
PACK = OFF_DW + N_DW
PACKQ = N_QKV + N_WO


class Ctr:
    __slots__ = ("sem", "v")

    def __init__(self, sem):
        self.sem = sem
        self.v = 0


class Lanes:

    def __init__(self, mk, name, n):
        self.cs = [mk(f"{name}{i}") for i in range(n)]
        self.i = 0

    def nxt(self):
        c = self.cs[self.i]
        self.i = (self.i + 1) % len(self.cs)
        return c

    def cur(self):
        return [Prog.tk(c) for c in self.cs]


class Prog:
    ENGINES = ("sync", "gpsimd", "vector", "scalar", "tensor")

    def __init__(self):
        self.ops = {e: [] for e in self.ENGINES}

    def emit(self, eng, fn, waits=(), inc=None, n=1):
        waits = tuple((c, v) for (c, v) in waits if c is not None and v > 0)
        self.ops[eng].append((fn, waits, None if inc is None else inc.sem, n))
        if inc is not None:
            inc.v += n
            return (inc, inc.v)
        return None

    @staticmethod
    def tk(ctr):
        return (ctr, ctr.v)

    def materialize(self, block):
        for eng in self.ENGINES:
            ops = self.ops[eng]
            if not ops:
                continue

            def run(e, ops=ops):
                seen = {}
                for fn, waits, sem, n in ops:
                    for (ctr, val) in waits:
                        if seen.get(id(ctr), -1) < val:
                            e.wait_ge(ctr.sem, val)
                            seen[id(ctr)] = val
                    inst = fn(e)
                    if sem is not None:
                        if inst is None:
                            inst = e.nop()
                        inst.then_inc(sem, n)

            getattr(block, eng)(run)


def _bcv(t, F):
    return t[:, 0 : F // 32].rearrange("p (b e) -> p b e", e=1).broadcast_to([128, F // 32, 32])


def _blkv(t, F):
    return t[:, 0:F].rearrange("p (b e) -> p b e", e=32)



def _psb(t, a, b):
    return t[:, a : b].bitcast(BF16)[:, 0 : b - a]


def build(stage="full", debug=0):
    nc = bass.Bass("TRN2", target_bir_lowering=False, detect_race_conditions=False)

    din = lambda n, s, dt=F32: nc.dram_tensor(n, s, dt, kind="ExternalInput")
    dou = lambda n, s, dt=F32: nc.dram_tensor(n, s, dt, kind="ExternalOutput")
    dit = lambda n, s, dt=F32: nc.dram_tensor(n, s, dt)

    x_in = din("x", [T, D])
    wqkv_sh = din("wqkv_sh", [QSH, 3 * D])
    bqkv_in = din("bqkv", [1, 3 * D])
    wout_sh = din("wout_sh", [WOSH, D])
    bout_in = din("bout", [1, D])
    slopes_in = din("slopes", [1, H])
    g_sh = din("g_sh", [GSH, D])
    u_sh = din("u_sh", [GSH, D])
    dw_sh = din("dw_sh", [DSH, IF])
    c_in = din("c_in", [1, D])
    wmod_sh = din("wmod_sh", [D, WMSH])
    steps_in = din("steps", [1, NCORE])
    maskr_in = din("maskr", [1, NCORE])
    out_d = dou("out", [T, D])

    modsh_d = dit("modsh_d", [1, WMSH])
    modf_d = dit("modf_d", [1, NCORE * WMSH])
    gaeff_d = dit("gaeff_d", [1, D])
    packsh_d = dit("packsh_d", [1, PACK], BF16)
    packf_d = dit("packf_d", [NCORE, PACK], BF16)
    packqsh_d = dit("packqsh_d", [1, PACKQ], BF16)
    packqf_d = dit("packqf_d", [NCORE, PACKQ], BF16)
    dsh_d = dit("dsh_d", [H, HD, HD])
    dfull_d = dit("dfull_d", [NCORE, H, HD, HD])
    qsp_d = dit("qsp_d", [NTC, HD, D], BF16)
    ksp_d = dit("ksp_d", [NTC, HD, D], BF16)
    vsp_d = dit("vsp_d", [NTC, HD, D], BF16)
    x2sp_d = dit("x2sp_d", [T, D])
    hq_d = dit("hq_d", [IF // HD, HD, T], BF16)

    def pkflat(r, off, n):
        return packf_d[r : r + 1, off : off + n].squeeze(0)

    def pk2d(r, off, rows, cols):
        return pkflat(r, off, rows * cols).rearrange("(r c) -> r c", r=rows)

    def pkq2d(r, off, rows, cols):
        return packqf_d[r : r + 1, off : off + rows * cols].squeeze(0).rearrange(
            "(r c) -> r c", r=rows)

    def wqkv_bf(kc, c0, cn):
        return pkq2d(kc // 2, OFF_QKV + (kc % 2) * 128 * 3 * D, 128, 3 * D)[:, c0 : c0 + cn]

    def wout_bf(kc, c0, cn):
        return pkq2d(kc // 2, OFF_WO + (kc % 2) * 128 * D, 128, D)[:, c0 : c0 + cn]

    dbg = {}
    if debug >= 1:
        dbg["mod"] = dou("dbg_mod", [1, NCORE * WMSH])
        dbg["gdeq"] = dou("dbg_gdeq", [GSH, D], BF16)
        dbg["ddeq"] = dou("dbg_ddeq", [DSH, IF], BF16)
        dbg["garep"] = dou("dbg_garep", [1, D])
        dbg["gacol"] = dou("dbg_gacol", [128, 16])
        dbg["gtmp"] = dou("dbg_gtmp", [128, 16])
        dbg["gt2"] = dou("dbg_gt2", [128, 16])
        dbg["sq"] = dou("dbg_sq", [128, 16])
    if debug >= 2:
        dbg["q"] = dou("dbg_q", [NTC, HD, D], BF16)
        dbg["k"] = dou("dbg_k", [NTC, HD, D], BF16)
        dbg["v"] = dou("dbg_v", [NTC, HD, D], BF16)
        dbg["dloc"] = dou("dbg_dloc", [H, HD, HD])
        dbg["sinit"] = dou("dbg_sinit", [H, HD, HD])
    if debug >= 3:
        dbg["x2"] = dou("dbg_x2", [T, D])
        dbg["x2q"] = dou("dbg_x2q", [16, HD, T], BF16)
        dbg["h2"] = dou("dbg_h2", [T, D])
        dbg["eln"] = dou("dbg_eln", [NTC, 128, 2])
    if debug >= 4:
        dbg["hq"] = dou("dbg_hq", [IF // HD, HD, T], BF16)
    if debug >= 5:
        dbg["ey"] = dou("dbg_ey", [T, D])
        dbg["ee"] = dou("dbg_ee", [T, D], I32)
        dbg["eu"] = dou("dbg_eu", [T, D])
        dbg["ez"] = dou("dbg_ez", [T, D])
        dbg["er"] = dou("dbg_er", [T, D], I32)
        dbg["es"] = dou("dbg_es", [NTC, 128, 64])

    p = Prog()
    E = p.emit
    TK = p.tk

    with ExitStack() as top:
        sb = lambda n, sh, dt=F32: top.enter_context(nc.sbuf_tensor(n, sh, dt))
        mkc = lambda n: Ctr(top.enter_context(nc.semaphore(n)))

        QL = Lanes(mkc, "cQ", 6)
        GL = Lanes(mkc, "cG", 6)
        cGc = mkc("cGc")
        cV = mkc("cV")
        cS = mkc("cS")
        cT = mkc("cT")
        cO = mkc("cO")
        cOg = mkc("cOg")
        cC = mkc("cC")
        ALLC = tuple(QL.cs) + tuple(GL.cs) + (cGc, cV, cS, cT, cO, cOg, cC)

        def bar():
            return [TK(c) for c in ALLC]

        cp = sb("cpool", [128, 8])
        CLN2, CNLN2, CNB3, CPB3, CEPS = 0, 1, 2, 3, 4
        ident_f = sb("ident_f", [128, 128])
        ident_b = sb("ident_b", [128, 128], BF16)
        iota_d = sb("iota_d", [128, 128])
        slopes_rep = sb("slopes_rep", [128, H])
        nslopes = sb("nslopes", [128, H])
        maskT = sb("maskT", [128, H * 128], BF16)
        qdec = sb("qdec", [128, H * 128], BF16)
        kdcol = sb("kdcol", [128, H])
        bdcol = sb("bdcol", [128, H])
        kd2col = sb("kd2col", [128, NTC * H])
        ca1c = sb("ca1c", [128, 16])
        sac = sb("sac", [128, 16])
        gbcol = sb("gbcol", [128, 16])
        ga_rep = sb("ga_rep", [128, D], BF16)
        cb1_rep = sb("cb1_rep", [128, D], BF16)
        sb_rep = sb("sb_rep", [128, D], BF16)
        ones_row = sb("ones_row", [1, 512])
        bout_row = sb("bout_row", [1, D])
        coefs = sb("coefs", [128, NCORE * H])
        colp = sb("colp", [128, 2])

        with ExitStack() as pa:
            asb = lambda n, sh, dt=F32: pa.enter_context(nc.sbuf_tensor(n, sh, dt))
            scmod = asb("scmod", [128, 16])
            stepr = asb("stepr", [128, NCORE])
            maskrr = asb("maskrr", [128, NCORE])
            graw = asb("graw", [128, 16])
            gbraw = asb("gbraw", [128, 16])
            gtmp = asb("gtmp", [128, 16])
            mtmp = asb("mtmp", [128, H * 128])
            mmin = asb("mmin", [128, 128])
            wmt = [asb(f"wmt{i}", [128, WMSH]) for i in range(2)]
            modrow = asb("modrow", [1, WMSH])

            t_c = E("sync", lambda e: e.dma_start(
                out=scmod[:, :], in_=c_in[0:1, :].squeeze(0).rearrange("(k q) -> q k", q=128),
                allow_slow_non_contiguous=True),
              inc=QL.nxt(), n=16)
            t_bout = E("sync", lambda e: e.dma_start(out=bout_row[:, :], in_=bout_in[:, :]),
                       inc=QL.nxt(), n=16)

            for col, val in ((CLN2, LN2), (CNLN2, -LN2), (CNB3, -3.0 * LN2),
                             (CPB3, 3.0 * LN2), (CEPS, 1e-5)):
                E("gpsimd", lambda e, col=col, val=val: e.memset(cp[:, col : col + 1], val),
                  inc=cGc, n=1)
            E("gpsimd", lambda e: e.memset(ones_row[:, :], 1.0), inc=cGc, n=1)
            t_ones = TK(cGc)
            E("gpsimd", lambda e: e.iota(iota_d[:, :], [[1, 128]], base=0, channel_multiplier=-1,
                                         allow_small_or_imprecise_dtypes=True), inc=cGc, n=1)
            t_iota = TK(cGc)
            t_sl = E("gpsimd", lambda e: e.dma_start(
                out=slopes_rep[:, :], in_=slopes_in[0:1, :].partition_broadcast(128).squeeze(1)),
              inc=GL.nxt(), n=16)
            t_step1 = E("gpsimd", lambda e: e.dma_start(
                out=stepr[:, :], in_=steps_in[0:1, :].partition_broadcast(128).squeeze(1)),
              inc=GL.nxt(), n=16)
            t_step2 = E("gpsimd", lambda e: e.dma_start(
                out=maskrr[:, :], in_=maskr_in[0:1, :].partition_broadcast(128).squeeze(1)),
              inc=GL.nxt(), n=16)
            t_steps = [t_step1, t_step2]

            E("gpsimd", lambda e: e.memset(mtmp[:, 0:128], 1.0), inc=cGc, n=1)
            t_ones128 = TK(cGc)
            E("gpsimd", lambda e: e.affine_select(ident_f[:, :], mtmp[:, 0:128], [[1, 128]],
                                                  compare_op=Alu.is_equal, fill=0.0, base=0,
                                                  channel_multiplier=-1),
              waits=[t_ones128], inc=cGc, n=1)
            t_identg = TK(cGc)
            E("vector", lambda e: e.tensor_copy(ident_b[:, :], ident_f[:, :]),
              waits=[t_identg], inc=cV, n=1)
            E("vector", lambda e: e.tensor_copy(ident_f[:, :], ident_b[:, :]), inc=cV, n=1)
            t_ident = TK(cV)
            E("vector", lambda e: e.tensor_scalar(nslopes[:, :], slopes_rep[:, :], -1.0, None,
                                                  Alu.mult), waits=[t_sl], inc=cV, n=1)
            t_nsl = TK(cV)
            E("vector", lambda e: e.tensor_scalar(colp[:, 0:1], iota_d[:, 0:1], -1.0, None,
                                                  Alu.mult), waits=[t_iota], inc=cV, n=1)
            t_colp = TK(cV)

            E("scalar", lambda e: e.activation(scmod[:, :], scmod[:, :], Act.Silu),
              waits=[t_c], inc=cS, n=1)
            t_scmod = TK(cS)

            with nc.psum_tensor("pmod", [1, 3 * 512], F32) as pmod:
                mm_t = []
                for kc in range(16):
                    w = [mm_t[kc - 2]] if kc >= 2 else []
                    t_wm = E("sync", lambda e, kc=kc: e.dma_start(
                        out=wmt[kc % 2][:, :], in_=wmod_sh[kc * 128 : (kc + 1) * 128, :]),
                      waits=w, inc=QL.nxt(), n=16)

                    def mmf(e, kc=kc):
                        r = None
                        for n3 in range(3):
                            r = e.matmul(pmod[0:1, n3 * 512 : (n3 + 1) * 512],
                                         scmod[:, kc : kc + 1],
                                         wmt[kc % 2][:, n3 * 512 : (n3 + 1) * 512],
                                         start=(kc == 0), stop=(kc == 15))
                        return r
                    E("tensor", mmf, waits=[t_wm, t_scmod], inc=cT, n=1)
                    mm_t.append(TK(cT))
                E("vector", lambda e: e.tensor_copy(modrow[:, :], pmod[0:1, 0:WMSH]),
                  waits=[mm_t[-1]], inc=cV, n=1)
                t_modrow = TK(cV)
            t_modst = E("gpsimd", lambda e: e.dma_start(out=modsh_d[:, :], in_=modrow[:, :]),
                        waits=[t_modrow], inc=GL.nxt(), n=16)
            E("gpsimd", lambda e: e.collective_compute(
                "AllGather", Alu.bypass, replica_groups=[list(range(NCORE))],
                ins=[modsh_d.ap().opt()], outs=[modf_d.ap().opt()]),
              waits=[t_modst], inc=cC, n=1)
            t_modag = TK(cC)
            if debug >= 1:
                E("gpsimd", lambda e: e.dma_start(out=dbg["mod"][:, :], in_=modf_d[:, :]),
                  waits=[t_modag], inc=cOg, n=16)

            def modcol(dst, off):
                return E("gpsimd", lambda e, dst=dst, off=off: e.dma_start(
                    out=dst,
                    in_=modf_d[0:1, off : off + D].squeeze(0).rearrange("(k q) -> q k", q=128),
                    allow_slow_non_contiguous=True),
                  waits=[t_modag], inc=GL.nxt(), n=16)

            t_modcols = []
            t_modcols.append(modcol(sac[:, :], 0))
            t_modcols.append(modcol(ca1c[:, :], D))
            t_modcols.append(modcol(graw[:, :], 2 * D))
            t_modcols.append(modcol(gbraw[:, :], 5 * D))
            t_modcols.append(E("gpsimd", lambda e: e.dma_start(
                out=cb1_rep[:, :],
                in_=modf_d[0:1, 4 * D : 5 * D].partition_broadcast(128).squeeze(1)),
              waits=[t_modag], inc=GL.nxt(), n=16))
            t_modcols.append(E("gpsimd", lambda e: e.dma_start(
                out=sb_rep[:, :],
                in_=modf_d[0:1, 3 * D : 4 * D].partition_broadcast(128).squeeze(1)),
              waits=[t_modag], inc=GL.nxt(), n=16))

            E("vector", lambda e: e.tensor_scalar(ca1c[:, :], ca1c[:, :], 1.0, None, Alu.add),
              waits=t_modcols, inc=cV, n=1)
            E("vector", lambda e: e.tensor_scalar(cb1_rep[:, :], cb1_rep[:, :], 1.0, None, Alu.add),
              inc=cV, n=1)
            t_mods = TK(cV)

            gt2 = scmod
            for raw, dstc in ((graw, graw), (gbraw, gbcol)):
                E("scalar", lambda e, raw=raw: e.activation(gtmp[:, :], raw[:, :], Act.Abs),
                  waits=t_modcols + [mm_t[-1], TK(cV)], inc=cS, n=1)
                E("scalar", lambda e, raw=raw: e.activation(gt2[:, :], raw[:, :], Act.Sign),
                  inc=cS, n=1)
                E("scalar", lambda e, raw=raw: e.activation(raw[:, :], gt2[:, :], Act.Square),
                  inc=cS, n=1)
                t_abs = TK(cS)
                if debug >= 1 and raw is graw:
                    E("gpsimd", lambda e: e.dma_start(out=dbg["gtmp"][:, :], in_=gtmp[:, :]),
                      waits=[t_abs], inc=cOg, n=16)
                    E("gpsimd", lambda e: e.dma_start(out=dbg["gt2"][:, :], in_=gt2[:, :]),
                      waits=[t_abs], inc=cOg, n=16)
                    E("gpsimd", lambda e: e.dma_start(out=dbg["sq"][:, :], in_=graw[:, :]),
                      waits=[t_abs], inc=cOg, n=16)
                E("vector", lambda e: e.tensor_scalar(gtmp[:, :], gtmp[:, :], 0.25, None, Alu.max),
                  waits=[t_abs], inc=cV, n=1)
                E("vector", lambda e, raw=raw: e.tensor_scalar(raw[:, :], raw[:, :], -1.0, 1.0,
                                                               Alu.mult, Alu.add),
                  waits=[TK(cV)], inc=cV, n=1)
                E("vector", lambda e, raw=raw: e.tensor_tensor(raw[:, :], raw[:, :], gt2[:, :],
                                                               Alu.add), waits=[TK(cV)], inc=cV, n=1)
                E("vector", lambda e, raw=raw, dstc=dstc: e.tensor_tensor(
                    dstc[:, :], raw[:, :], gtmp[:, :], Alu.mult), waits=[TK(cV)], inc=cV, n=1)
            t_geff = TK(cV)

            t_gast = E("gpsimd", lambda e: e.dma_start(
                out=gaeff_d[0:1, :].squeeze(0).rearrange("(k q) -> q k", q=128), in_=graw[:, :],
                allow_slow_non_contiguous=True),
              waits=[t_geff], inc=GL.nxt(), n=16)
            if debug >= 1:
                E("gpsimd", lambda e: e.dma_start(out=dbg["gacol"][:, :], in_=graw[:, :]),
                  waits=[t_geff], inc=cOg, n=16)
            t_garep = E("gpsimd", lambda e: e.dma_start(
                out=ga_rep[:, :], in_=gaeff_d[0:1, :].partition_broadcast(128).squeeze(1)),
              waits=[t_gast], inc=GL.nxt(), n=16)
            if debug >= 1:
                E("gpsimd", lambda e: e.dma_start(out=dbg["garep"][:, :], in_=gaeff_d[:, :]),
                  waits=[t_gast], inc=cOg, n=16)

            E("vector", lambda e: e.tensor_scalar(mmin[:, :], iota_d[:, :], 0.0, 10000.0,
                                                  Alu.min, Alu.mult),
              waits=[t_iota, t_geff], inc=cV, n=1)
            t_mneg = TK(cV)
            for h in range(H):
                E("vector", lambda e, h=h: e.scalar_tensor_tensor(
                    mtmp[:, h * 128 : (h + 1) * 128], iota_d[:, :],
                    nslopes[:, h : h + 1], mmin[:, :], Alu.mult, Alu.add),
                  waits=[t_mneg, t_nsl], inc=cV, n=1)
            t_m1 = TK(cV)
            E("scalar", lambda e: e.activation(maskT[:, :], mtmp[:, :], Act.Exp),
              waits=[t_m1], inc=cS, n=1)
            t_maskT = TK(cS)
            for h in range(H):
                E("vector", lambda e, h=h: e.scalar_tensor_tensor(
                    mtmp[:, h * 128 : (h + 1) * 128], iota_d[:, :], 1.0,
                    colp[:, 0:1].broadcast_to([128, 1, 128]).squeeze(1), Alu.mult, Alu.add),
                  waits=[t_colp, t_maskT], inc=cV, n=1)
                E("vector", lambda e, h=h: e.tensor_scalar(
                    mtmp[:, h * 128 : (h + 1) * 128], mtmp[:, h * 128 : (h + 1) * 128],
                    nslopes[:, h : h + 1], None, Alu.mult), inc=cV, n=1)
            t_q1 = TK(cV)
            E("scalar", lambda e: e.activation(qdec[:, :], mtmp[:, :], Act.Exp),
              waits=[t_q1], inc=cS, n=1)
            t_qdec = TK(cS)
            for h in range(H):
                E("vector", lambda e, h=h: e.tensor_scalar(kdcol[:, h : h + 1], colp[:, 0:1],
                                                           -128.0, None, Alu.add),
                  waits=[t_colp, TK(cV)], inc=cV, n=1)
                E("vector", lambda e, h=h: e.tensor_scalar(kdcol[:, h : h + 1],
                                                           kdcol[:, h : h + 1],
                                                           slopes_rep[:, h : h + 1], None,
                                                           Alu.mult), waits=[TK(cV)], inc=cV, n=1)
            E("vector", lambda e: e.tensor_scalar(bdcol[:, :], slopes_rep[:, :], -128.0, None,
                                                  Alu.mult), waits=[t_sl], inc=cV, n=1)
            for tc in range(NTC):
                E("vector", lambda e, tc=tc: e.tensor_scalar(colp[:, 1:2], colp[:, 0:1],
                                                             float(128 * tc - T), None, Alu.add),
                  waits=[TK(cV)], inc=cV, n=1)
                t_c12 = TK(cV)
                for h in range(H):
                    E("vector", lambda e, tc=tc, h=h: e.tensor_scalar(
                        kd2col[:, tc * H + h : tc * H + h + 1], colp[:, 1:2],
                        slopes_rep[:, h : h + 1], None, Alu.mult),
                      waits=[t_c12], inc=cV, n=1)
            E("vector", lambda e: e.tensor_scalar(gtmp[:, 0:H], slopes_rep[:, :], -2048.0, None,
                                                  Alu.mult), waits=[TK(cV)], inc=cV, n=1)
            t_ns2048 = TK(cV)
            for j in range(NCORE):
                E("vector", lambda e, j=j: e.tensor_tensor(
                    coefs[:, j * H : (j + 1) * H], gtmp[:, 0:H],
                    stepr[:, j : j + 1].broadcast_to([128, 1, H]).squeeze(1), Alu.mult),
                  waits=t_steps + [t_ns2048], inc=cV, n=1)
            t_dec1 = TK(cV)
            E("scalar", lambda e: e.activation(kdcol[:, :], kdcol[:, :], Act.Exp),
              waits=[t_dec1], inc=cS, n=1)
            E("scalar", lambda e: e.activation(bdcol[:, :], bdcol[:, :], Act.Exp), inc=cS, n=1)
            E("scalar", lambda e: e.activation(kd2col[:, :], kd2col[:, :], Act.Exp), inc=cS, n=1)
            E("scalar", lambda e: e.activation(coefs[:, :], coefs[:, :], Act.Exp), inc=cS, n=1)
            t_dec2 = TK(cS)
            for j in range(NCORE):
                E("vector", lambda e, j=j: e.tensor_tensor(
                    coefs[:, j * H : (j + 1) * H], coefs[:, j * H : (j + 1) * H],
                    maskrr[:, j : j + 1].broadcast_to([128, 1, H]).squeeze(1), Alu.mult),
                  waits=[t_dec2], inc=cV, n=1)
            t_coefs = TK(cV)
            barA = bar()

        with ExitStack() as pb:
            bsb = lambda n, sh, dt=F32: pb.enter_context(nc.sbuf_tensor(n, sh, dt))
            win = [bsb(f"bwin{i}", [128, D]) for i in range(2)]
            bA = bsb("bA", [128, D])
            bB = bsb("bB", [128, D])
            bC = bsb("bC", [128, D])
            bEi = bsb("bEi", [128, D], I32)
            bdq = [bsb(f"bdq{i}", [128, D], BF16) for i in range(2)]
            bmx = bsb("bmx", [128, D // 32])
            bs_ = bsb("bs", [128, D // 32])
            brs = bsb("brs", [128, D // 32])

            jobs = []
            for i in range(GSH // 128):
                jobs.append((g_sh, i * 128, 0, D, OFF_G + i * 128 * D))
            for i in range(GSH // 128):
                jobs.append((u_sh, i * 128, 0, D, OFF_U + i * 128 * D))
            for i in range(DSH // 128):
                for cchunk in range(IF // D):
                    jobs.append((dw_sh, i * 128, cchunk * D, IF,
                                 OFF_DW + i * 128 * IF))

            if debug >= 1:
                dbg["win0"] = dou("dbg_win0", [128, D])

            cjobs = []
            for i in range(QSH // 128):
                cjobs.append((wqkv_sh, i * 128, 3 * D, OFF_QKV + i * 128 * 3 * D))
            for i in range(WOSH // 128):
                cjobs.append((wout_sh, i * 128, D, OFF_WO + i * 128 * D))
            for (src, r0, F, off) in cjobs:
                E("gpsimd", lambda e, src=src, r0=r0, F=F, off=off: e.dma_start(
                    out=packqsh_d[0:1, off : off + 128 * F].squeeze(0).rearrange(
                        "(q f) -> q f", q=128),
                    in_=src[r0 : r0 + 128, :]), inc=GL.nxt(), n=16)
            E("gpsimd", lambda e: e.collective_compute(
                "AllGather", Alu.bypass, replica_groups=[list(range(NCORE))],
                ins=[packqsh_d.ap().opt()], outs=[packqf_d.ap().opt()]),
              waits=GL.cur(), inc=cC, n=1)
            t_packagQ = TK(cC)

            st_t = {}
            for j, (src, r0, c0, rowlen, off) in enumerate(jobs):
                F = D
                w = list(barA) if j < 2 else [st_t[j - 2]]
                t_ld = E("sync", lambda e, src=src, r0=r0, c0=c0, j=j: e.dma_start(
                    out=win[j % 2][:, 0:D], in_=src[r0 : r0 + 128, c0 : c0 + D]),
                  waits=w, inc=QL.nxt(), n=16)
                if j == 0 and debug >= 1:
                    E("gpsimd", lambda e: e.dma_start(out=dbg["win0"][:, :], in_=win[0][:, :]),
                      waits=[t_ld], inc=cOg, n=16)
                wt = win[j % 2]
                NB = F // 32
                E("vector", lambda e, wt=wt, F=F, NB=NB: e.tensor_reduce(
                    bmx[:, 0:NB].rearrange("p (b e) -> p b e", e=1), _blkv(wt, F), AX, Alu.max,
                    apply_absolute_value=True), waits=[t_ld] + list(barA), inc=cV, n=1)
                E("vector", lambda e, NB=NB: e.tensor_scalar(bs_[:, 0:NB], bmx[:, 0:NB], INV448,
                                                             1e-30, Alu.mult, Alu.max),
                  waits=[TK(cV)], inc=cV, n=1)
                E("vector", lambda e, NB=NB: e.reciprocal(brs[:, 0:NB], bs_[:, 0:NB]),
                  waits=[TK(cV)], inc=cV, n=1)
                E("vector", lambda e, wt=wt, F=F: e.tensor_tensor(
                    _blkv(bA, F), _blkv(wt, F), _bcv(brs, F), Alu.mult),
                  waits=[TK(cV)], inc=cV, n=1)
                t_y = TK(cV)
                E("scalar", lambda e, F=F: e.activation(bB[:, 0:F], bA[:, 0:F], Act.Square),
                  waits=[t_y] + list(barA), inc=cS, n=1)
                t_sq = TK(cS)
                E("vector", lambda e, F=F: e.tensor_scalar(bB[:, 0:F], bB[:, 0:F], 0.000244140625,
                                                           None, Alu.max), waits=[t_sq],
                  inc=cV, n=1)
                t_cl = TK(cV)
                E("scalar", lambda e, F=F: e.activation(bB[:, 0:F], bB[:, 0:F], Act.Ln),
                  waits=[t_cl], inc=cS, n=1)
                t_ln = TK(cS)
                E("vector", lambda e, F=F: e.tensor_scalar(bEi[:, 0:F], bB[:, 0:F], INV_2LN2,
                                                           -0.5, Alu.mult, Alu.add),
                  waits=[t_ln], inc=cV, n=1)
                t_ei = TK(cV)
                E("scalar", lambda e, F=F: e.activation(bB[:, 0:F], bEi[:, 0:F], Act.Exp,
                                                        bias=cp[:, CNB3 : CNB3 + 1],
                                                        scale=cp[:, CLN2 : CLN2 + 1]),
                  waits=[t_ei], inc=cS, n=1)
                E("scalar", lambda e, F=F: e.activation(bC[:, 0:F], bEi[:, 0:F], Act.Exp,
                                                        bias=cp[:, CPB3 : CPB3 + 1],
                                                        scale=cp[:, CNLN2 : CNLN2 + 1]),
                  inc=cS, n=1)
                t_exp = TK(cS)
                E("vector", lambda e, F=F: e.tensor_tensor(bC[:, 0:F], bA[:, 0:F], bC[:, 0:F],
                                                           Alu.mult), waits=[t_exp], inc=cV, n=1)
                E("vector", lambda e, F=F: e.tensor_copy(bEi[:, 0:F], bC[:, 0:F]), inc=cV, n=1)
                E("vector", lambda e, F=F: e.tensor_tensor(bC[:, 0:F], bEi[:, 0:F], bB[:, 0:F],
                                                           Alu.mult), inc=cV, n=1)
                wdq = [st_t[j - 2]] if j >= 2 else []
                E("vector", lambda e, F=F, j=j: e.tensor_tensor(_blkv(bdq[j % 2], F), _blkv(bC, F),
                                                           _bcv(bs_, F), Alu.mult),
                  waits=wdq, inc=cV, n=1)
                t_dq = TK(cV)
                st_t[j] = E("gpsimd", lambda e, off=off, c0=c0, rowlen=rowlen, j=j: e.dma_start(
                    out=packsh_d[0:1, off : off + 128 * rowlen].squeeze(0).rearrange(
                        "(q f) -> q f", q=128)[:, c0 : c0 + D],
                    in_=bdq[j % 2][:, 0:D]), waits=[t_dq], inc=GL.nxt(), n=16)

            barB = bar()
            E("gpsimd", lambda e: e.collective_compute(
                "AllGather", Alu.bypass, replica_groups=[list(range(NCORE))],
                ins=[packsh_d.ap().opt()], outs=[packf_d.ap().opt()]),
              waits=GL.cur(), inc=cC, n=1)
            t_packag = TK(cC)

        if debug >= 1:
            E("gpsimd", lambda e: e.dma_start(out=dbg["gdeq"][:, :], in_=pk2d(0, OFF_G, GSH, D)),
              waits=[t_packag], inc=cOg, n=16)
            E("gpsimd", lambda e: e.dma_start(out=dbg["ddeq"][:, :], in_=pk2d(0, OFF_DW, DSH, IF)),
              waits=[t_packag], inc=cOg, n=16)

        def finish(extra):
            E("sync", lambda e: None, waits=list(extra) + bar())
            blk = top.enter_context(nc.Block("all"))
            p.materialize(blk)

        if stage == "B":
            E("sync", lambda e: e.dma_start(out=out_d[:, :], in_=x_in[:, :]),
              waits=[t_packag], inc=cO, n=16)
            finish([])
            return nc

        t_spq = {}
        t_spk = {}
        t_spv = {}
        t_dag = None
        t_x2sp = {}
        t_x2q_last = None

        with ExitStack() as xs:
            x2q = xs.enter_context(nc.sbuf_tensor("x2q", [128, 16 * T], BF16))

            def x2q_sl(kc, a, b):
                return x2q[:, kc * T + a : kc * T + b]

            ccut = 5
            if stage.startswith("C") and len(stage) > 1:
                ccut = int(stage[1:])
            c_ng = (0 if ccut == 0 else 1) if ccut <= 3 else NG
            with ExitStack() as pc:
                csb = lambda n, sh, dt=F32: pc.enter_context(nc.sbuf_tensor(n, sh, dt))
                xt = [csb(f"cxt{i}", [128, D]) for i in range(2)]
                sqs = csb("csqs", [128, D])
                h1 = csb("ch1", [128, 16 * 512], BF16)
                wq = [csb(f"cwq{i}", [128, 1024], BF16) for i in range(2)]
                kg = csb("ckg", [128, H * 512], BF16)
                vgf = csb("cvgf", [128, H * 512], BF16)
                vgt = csb("cvgt", [128, NG * D], BF16)
                stg = [csb(f"cstg{i}", [128, 512], BF16) for i in range(2)]
                kd2t = [csb(f"ckd2t{i}", [128, 128], BF16) for i in range(2)]
                Dsb = csb("cD", [128, H * 128])
                lnm = csb("clnm", [128, 1])
                lnr = csb("clnr", [128, 1])
                lna = csb("clna", [128, 1])
                lnb = csb("clnb", [128, 1])
                bqr = csb("cbqr", [1, 1024])
                pqk = pc.enter_context(nc.psum_tensor("pqk", [128, 8 * 512], F32))
                ptr = pqk
                pD_base = 4 * 512

                E("vector", lambda e: e.memset(Dsb[:, :], 0.0), waits=barB, inc=cV, n=1)
                t_Dadd = TK(cV)

                tr_free = [None] * 4
                t_x_free = [None, None]
                pqk_free = [None] * 8
                wq_use = [None, None]
                bqr_use = None
                kg_free = None
                vgf_free = None
                vgt_free = [None] * NG
                stg_free = [None, None]
                kd2_free = [None, None]
                pD_free = [None] * 4
                first_sync = barB
                first_sc = barB

                for g in range(c_ng):
                    qspl = []
                    kspl = []
                    h1_last = None
                    for tc4 in range(4):
                        tcg = g * 4 + tc4
                        xb = xt[tc4 % 2]
                        w = ([t_x_free[tc4 % 2]] if t_x_free[tc4 % 2] else []) + \
                            (first_sync or [])
                        t_x = E("sync", lambda e, tcg=tcg, xb=xb: e.dma_start(
                            out=xb[:, :], in_=x_in[tcg * 128 : (tcg + 1) * 128, :]),
                          waits=w, inc=QL.nxt(), n=16)
                        first_sync = None
                        E("vector", lambda e, xb=xb: e.reduce_sum(lna[:, :], xb[:, :], AX),
                          waits=[t_x], inc=cV, n=1)
                        t_sx = TK(cV)
                        E("scalar", lambda e, xb=xb: e.activation(sqs[:, :], xb[:, :], Act.Square,
                                                           accum_out=lnb[:, :]),
                          waits=[t_x] + (first_sc or []), inc=cS, n=1)
                        first_sc = None
                        t_sq2 = TK(cS)
                        E("vector", lambda e: e.tensor_scalar(lnm[:, :], lna[:, :], 1.0 / D, None,
                                                              Alu.mult), waits=[t_sx, TK(cV)],
                          inc=cV, n=1)
                        E("vector", lambda e: e.tensor_scalar(lnb[:, :], lnb[:, :], 1.0 / D, None,
                                                              Alu.mult), waits=[t_sq2, TK(cV)],
                          inc=cV, n=1)
                        E("vector", lambda e: e.scalar_tensor_tensor(
                            lna[:, :], lnm[:, :], lnm[:, 0:1], lnb[:, :], Alu.mult, Alu.subtract),
                          waits=[TK(cV)], inc=cV, n=1)
                        E("vector", lambda e: e.tensor_scalar(lna[:, :], lna[:, :], -1.0, None,
                                                              Alu.mult), waits=[TK(cV)], inc=cV, n=1)
                        t_var = TK(cV)
                        E("scalar", lambda e: e.activation(lna[:, :], lna[:, :], Act.Sqrt,
                                                           bias=cp[:, CEPS : CEPS + 1]),
                          waits=[t_var], inc=cS, n=1)
                        t_std = TK(cS)
                        E("vector", lambda e: e.reciprocal(lnr[:, :], lna[:, :]),
                          waits=[t_std], inc=cV, n=1)
                        E("vector", lambda e, xb=xb: e.tensor_scalar(
                            xb[:, :], xb[:, :], lnm[:, 0:1],
                            lnr[:, 0:1], Alu.subtract,
                            Alu.mult), waits=[TK(cV)],
                          inc=cV, n=1)
                        t_xc = TK(cV)
                        for m in range(16):
                            s4 = m % 4
                            w = [t_xc, t_ident] + ([tr_free[s4]] if tr_free[s4] else [])
                            E("tensor", lambda e, m=m, s4=s4, xb=xb: e.transpose(
                                ptr[:, s4 * 512 : s4 * 512 + 128],
                                xb[:, m * 128 : (m + 1) * 128], ident_f[:, :]),
                              waits=w, inc=cT, n=1)
                            t_tr = TK(cT)
                            if m == 15:
                                t_x_free[tc4 % 2] = t_tr
                            E("vector", lambda e, m=m, s4=s4, tc4=tc4: e.tensor_scalar(
                                h1[:, m * 512 + tc4 * 128 : m * 512 + (tc4 + 1) * 128],
                                ptr[:, s4 * 512 : s4 * 512 + 128],
                                ca1c[:, m : m + 1], sac[:, m : m + 1], Alu.mult, Alu.add),
                              waits=[t_tr, t_mods], inc=cV, n=1)
                            tr_free[s4] = TK(cV)
                        h1_last = tr_free[3]
                    t_h1 = h1_last
                    if ccut == 1:
                        continue

                    for mb in range(6):
                        w = ([bqr_use] if bqr_use else [])
                        t_bq = E("sync", lambda e, mb=mb: e.dma_start(
                            out=bqr[:, :], in_=bqkv_in[0:1, mb * 1024 : (mb + 1) * 1024]),
                          waits=w, inc=QL.nxt(), n=16)

                        def qkinit(e, mb=mb):
                            r = None
                            for i8 in range(8):
                                r = e.matmul(pqk[:, i8 * 512 : (i8 + 1) * 512],
                                             bqr[0:1, i8 * 128 : (i8 + 1) * 128],
                                             ones_row[0:1, 0:512],
                                             start=True, stop=False)
                            return r
                        w = [t_bq, t_ones] + [t for t in pqk_free if t] + \
                            [t for t in tr_free if t] + [t for t in pD_free if t]
                        E("tensor", qkinit, waits=w, inc=cT, n=1)
                        bqr_use = TK(cT)
                        for kc in range(16):
                            t_wq = E("sync", lambda e, kc=kc, mb=mb: e.dma_start(
                                out=wq[kc % 2][:, :], in_=wqkv_bf(kc, mb * 1024, 1024)),
                              waits=[t_packagQ] + ([wq_use[kc % 2]] if wq_use[kc % 2] else []),
                              inc=QL.nxt(), n=16)

                            def qkmm(e, kc=kc):
                                r = None
                                for i8 in range(8):
                                    r = e.matmul(pqk[:, i8 * 512 : (i8 + 1) * 512],
                                                 wq[kc % 2][:, i8 * 128 : (i8 + 1) * 128],
                                                 h1[:, kc * 512 : (kc + 1) * 512],
                                                 start=False, stop=(kc == 15))
                                return r
                            E("tensor", qkmm, waits=[t_wq, t_h1], inc=cT, n=1)
                            wq_use[kc % 2] = TK(cT)
                        t_stop = TK(cT)
                        for i8 in range(8):
                            m = mb * 8 + i8
                            if m < 16:
                                s2 = i8 % 2
                                w = [t_stop] + ([stg_free[s2]] if stg_free[s2] else [])
                                E("vector", lambda e, i8=i8, s2=s2: e.tensor_copy(
                                    stg[s2][:, :], pqk[:, i8 * 512 : (i8 + 1) * 512]),
                                  waits=w, inc=cV, n=1)
                                t_ev = TK(cV)
                                pqk_free[i8] = t_ev
                                stg_free[s2] = E(
                                    "gpsimd", lambda e, g=g, m=m, s2=s2: e.dma_start(
                                        out=qsp_d[g * 4 : (g + 1) * 4, :,
                                                  m * 128 : (m + 1) * 128]
                                        .rearrange("t p j -> p t j"),
                                        in_=stg[s2][:, :].rearrange("p (t j) -> p t j", t=4)),
                                    waits=[t_ev], inc=GL.nxt(), n=16)
                                qspl.append(stg_free[s2])
                            elif m < 32:
                                mh = m - 16
                                w = [t_stop] + (kg_free or []) + \
                                    ([vgf_free] if vgf_free else [])
                                E("vector", lambda e, i8=i8, mh=mh: e.tensor_copy(
                                    kg[:, mh * 512 : (mh + 1) * 512],
                                    pqk[:, i8 * 512 : (i8 + 1) * 512]), waits=w, inc=cV, n=1)
                                pqk_free[i8] = TK(cV)
                                kspl.append(E(
                                    "gpsimd", lambda e, g=g, mh=mh: e.dma_start(
                                        out=ksp_d[g * 4 : (g + 1) * 4, :,
                                                  mh * 128 : (mh + 1) * 128]
                                        .rearrange("t p j -> p t j"),
                                        in_=kg[:, mh * 512 : (mh + 1) * 512].rearrange(
                                            "p (t j) -> p t j", t=4)),
                                    waits=[pqk_free[i8]], inc=GL.nxt(), n=16))
                            else:
                                mv = m - 32
                                w = [t_stop] + ([vgf_free] if vgf_free else [])
                                E("vector", lambda e, i8=i8, mv=mv: e.tensor_copy(
                                    vgf[:, mv * 512 : (mv + 1) * 512],
                                    pqk[:, i8 * 512 : (i8 + 1) * 512]), waits=w, inc=cV, n=1)
                                pqk_free[i8] = TK(cV)
                    t_kg = TK(cV)
                    for tcg in range(g * 4, (g + 1) * 4):
                        t_spq[tcg] = qspl
                        t_spk[tcg] = kspl
                    if ccut == 2:
                        continue

                    for h in range(H):
                        t_dmm = None
                        for tc4 in range(4):
                            tcg = g * 4 + tc4
                            sv = tc4 % 4
                            w = [t_kg, t_ident] + ([tr_free[sv]] if tr_free[sv] else [])
                            E("tensor", lambda e, h=h, tc4=tc4, sv=sv: e.transpose(
                                _psb(ptr, sv * 512, sv * 512 + 128),
                                vgf[:, h * 512 + tc4 * 128 : h * 512 + (tc4 + 1) * 128],
                                ident_b[:, :]), waits=w, inc=cT, n=1)
                            t_vtr = TK(cT)
                            w = [t_vtr] + ([vgt_free[tc4]] if vgt_free[tc4] else [])
                            E("vector", lambda e, h=h, tc4=tc4, sv=sv: e.tensor_copy(
                                vgt[:, tc4 * D + h * 128 : tc4 * D + (h + 1) * 128],
                                _psb(ptr, sv * 512, sv * 512 + 128)), waits=w, inc=cV, n=1)
                            t_vt = TK(cV)
                            tr_free[sv] = t_vt
                            s2 = tc4 % 2
                            w = [t_kg, t_ident] + ([tr_free[sv]] if tr_free[sv] else [])
                            E("tensor", lambda e, h=h, tc4=tc4, sv=sv: e.transpose(
                                _psb(ptr, sv * 512, sv * 512 + 128),
                                kg[:, h * 512 + tc4 * 128 : h * 512 + (tc4 + 1) * 128],
                                ident_b[:, :]), waits=w, inc=cT, n=1)
                            t_ktr = TK(cT)
                            w = [t_ktr, t_dec2] + ([kd2_free[s2]] if kd2_free[s2] else [])
                            E("vector", lambda e, h=h, tcg=tcg, sv=sv, s2=s2: e.tensor_scalar(
                                kd2t[s2][:, :], _psb(ptr, sv * 512, sv * 512 + 128),
                                kd2col[:, tcg * H + h : tcg * H + h + 1], None, Alu.mult),
                              waits=w, inc=cV, n=1)
                            t_kd2 = TK(cV)
                            tr_free[sv] = t_kd2
                            sD = h % 4
                            w = [t_kd2, t_vt]
                            if tc4 == 0 and pD_free[sD]:
                                w.append(pD_free[sD])
                            E("tensor", lambda e, h=h, tc4=tc4, s2=s2, sD=sD: e.matmul(
                                pqk[:, pD_base + sD * 512 : pD_base + sD * 512 + 128],
                                kd2t[s2][:, :],
                                vgt[:, tc4 * D + h * 128 : tc4 * D + (h + 1) * 128],
                                start=(tc4 == 0), stop=(tc4 == 3)), waits=w, inc=cT, n=1)
                            t_dmm = TK(cT)
                            kd2_free[s2] = t_dmm
                        E("vector", lambda e, h=h, sD=h % 4: e.scalar_tensor_tensor(
                            Dsb[:, h * 128 : (h + 1) * 128],
                            pqk[:, pD_base + sD * 512 : pD_base + sD * 512 + 128],
                            0.0, Dsb[:, h * 128 : (h + 1) * 128], Alu.bypass, Alu.add),
                          waits=[t_dmm, t_Dadd], inc=cV, n=1)
                        t_Dadd = TK(cV)
                        pD_free[h % 4] = t_Dadd
                    vgf_free = t_Dadd
                    for tc4 in range(4):
                        tcg = g * 4 + tc4
                        t_spv[tcg] = E("gpsimd", lambda e, tcg=tcg, tc4=tc4: e.dma_start(
                            out=vsp_d[tcg, :, :], in_=vgt[:, tc4 * D : (tc4 + 1) * D]),
                          waits=[t_Dadd], inc=GL.nxt(), n=16)
                        vgt_free[tc4] = t_spv[tcg]
                    kg_free = kspl

                t_dst = E("gpsimd", lambda e: e.dma_start(
                    out=dsh_d[:, :, :].rearrange("h p f -> p h f"),
                    in_=Dsb[:, :].rearrange("p (h f) -> p h f", h=H)),
                  waits=[t_Dadd], inc=GL.nxt(), n=16)
                if ccut >= 5:
                    E("gpsimd", lambda e: e.collective_compute(
                        "AllGather", Alu.bypass, replica_groups=[list(range(NCORE))],
                        ins=[dsh_d.ap().opt()], outs=[dfull_d.ap().opt()]),
                      waits=[t_dst], inc=cC, n=1)
                    t_dag = TK(cC)
                if debug >= 2 and ccut >= 5:
                    E("gpsimd", lambda e: e.dma_start(out=dbg["dloc"][:, :, :],
                                                      in_=dsh_d[:, :, :]),
                      waits=[t_dst], inc=cOg, n=16)
                    for nm, src in (("q", qsp_d), ("k", ksp_d), ("v", vsp_d)):
                        E("gpsimd", lambda e, nm=nm, src=src: e.dma_start(
                            out=dbg[nm][:, :, :], in_=src[:, :, :]), waits=GL.cur(),
                          inc=cOg, n=16)
                barC = bar()

            if stage.startswith("C"):
                E("sync", lambda e: e.dma_start(out=out_d[:, :], in_=x_in[:, :]),
                  waits=barC, inc=cO, n=16)
                finish([])
                return nc

            with ExitStack() as pe:
                esb = lambda n, sh, dt=F32: pe.enter_context(nc.sbuf_tensor(n, sh, dt))
                S = esb("eS", [128, H * 128])
                qf = [esb(f"eqf{i}", [128, D], BF16) for i in range(2)]
                kf = [esb(f"ekf{i}", [128, D], BF16) for i in range(2)]
                vf = [esb(f"evf{i}", [128, D], BF16) for i in range(2)]
                Og = esb("eOg", [128, H * 512], BF16)
                sTm = [esb(f"esTm{i}", [128, 128], BF16) for i in range(2)]
                qdb = [esb(f"eqdb{i}", [128, 128]) for i in range(2)]
                kdt = [esb(f"ekdt{i}", [128, 128], BF16) for i in range(2)]
                xe2 = [esb(f"exe{i}", [128, D]) for i in range(2)]
                x2b2 = [esb(f"ex2b{i}", [128, D]) for i in range(2)]
                eC_ = esb("eeC", [128, D])
                eEi = esb("eeEi", [128, D], I32)
                x2qtm = esb("ex2qtm", [128, D], BF16)
                wo = [esb(f"ewo{i}", [128, 1024], BF16) for i in range(2)]
                emx = esb("eemx", [128, 64])
                es_ = esb("ees", [128, 64])
                ers = esb("eers", [128, 64])
                elnm = esb("eelnm", [128, 1])
                elnr = esb("eelnr", [128, 1])
                elna = esb("eelna", [128, 1])
                elnb = esb("eelnb", [128, 1])
                psc = pe.enter_context(nc.psum_tensor("psc", [128, 512], F32))
                ptrE = pe.enter_context(nc.psum_tensor("ptrE", [128, 512], F32))
                po = pe.enter_context(nc.psum_tensor("po", [128, 512], F32))
                pa = pe.enter_context(nc.psum_tensor("pa", [128, 512], F32))
                pop = pe.enter_context(nc.psum_tensor("pop", [128, 4 * 512], F32))

                E("vector", lambda e: e.memset(S[:, :], 0.0), waits=barC, inc=cV, n=1)
                t_S = TK(cV)
                dj_use = [None, None]
                for j in range(NCORE):
                    buf = xe2[j % 2]
                    w = [t_dag] + barC + ([dj_use[j % 2]] if dj_use[j % 2] else [])
                    t_dj = E("sync", lambda e, j=j, buf=buf: e.dma_start(
                        out=buf[:, 0 : H * 128].rearrange("p (h f) -> p h f", h=H),
                        in_=dfull_d[j, :, :, :].rearrange("h p f -> p h f")),
                      waits=w, inc=QL.nxt(), n=16)
                    for h in range(H):
                        E("vector", lambda e, j=j, h=h, buf=buf: e.scalar_tensor_tensor(
                            S[:, h * 128 : (h + 1) * 128],
                            buf[:, h * 128 : (h + 1) * 128],
                            coefs[:, j * H + h : j * H + h + 1],
                            S[:, h * 128 : (h + 1) * 128], Alu.mult, Alu.add),
                          waits=[t_dj, t_S, t_coefs], inc=cV, n=1)
                        t_S = TK(cV)
                    dj_use[j % 2] = t_S
                if debug >= 2:
                    E("gpsimd", lambda e: e.dma_start(
                        out=dbg["sinit"][:, :, :].rearrange("h p f -> p h f"),
                        in_=S[:, :].rearrange("p (h f) -> p h f", h=H)),
                      waits=[t_S], inc=cOg, n=16)
                    t_S = TK(cOg)

                qkv_free = [None, None]
                slot_sc = [None] * 4
                slot_tr = [None] * 4
                slot_o = [None] * 4
                slot_a = [None] * 4
                sTm_free = [None, None]
                qdb_free = [None, None]
                kdt_free = [None, None]
                pop_free = [None] * 4
                wo_use = [None, None]
                Og_read = None
                xe_free = [[dj_use[0]], [dj_use[1]]]
                t_xe_d = [None, None]
                t_x2qtap = None
                t_elntap = None
                t_q5 = []
                t_q5p = None
                t_eCfree = None
                x2b_free = [None, None]
                t_Supd = t_S
                t_Supd_h = {}
                first_sc2 = barC

                for g in range(NG):
                    t_og = None
                    for tc4 in range(4):
                        tcg = g * 4 + tc4
                        b = tcg % 2
                        w = list(t_spq.get(tcg) or []) + list(t_spk.get(tcg) or []) + \
                            ([t_spv[tcg]] if tcg in t_spv else []) + \
                            ([qkv_free[b]] if qkv_free[b] else [])
                        t_q1 = E("sync", lambda e, tcg=tcg, b=b: e.dma_start(
                            out=qf[b][:, :], in_=qsp_d[tcg, :, :]), waits=w,
                            inc=QL.nxt(), n=16)
                        t_q2 = E("sync", lambda e, tcg=tcg, b=b: e.dma_start(
                            out=kf[b][:, :], in_=ksp_d[tcg, :, :]), inc=QL.nxt(), n=16)
                        t_q3 = E("sync", lambda e, tcg=tcg, b=b: e.dma_start(
                            out=vf[b][:, :], in_=vsp_d[tcg, :, :]), inc=QL.nxt(), n=16)
                        t_qkv = [t_q1, t_q2, t_q3]
                        last_use = None
                        for h in range(H):
                            s4 = 0
                            s2 = h % 2
                            hsl = slice(h * 128, (h + 1) * 128)
                            w = t_qkv + ([slot_sc[s4]] if slot_sc[s4] else [])
                            E("tensor", lambda e, b=b, hsl=hsl, s4=s4: e.matmul(
                                psc[:, s4 * 128 : (s4 + 1) * 128], kf[b][:, hsl], qf[b][:, hsl],
                                start=True, stop=True), waits=w, inc=cT, n=1)
                            t_sc = TK(cT)
                            w = [t_sc, t_maskT] + ([sTm_free[s2]] if sTm_free[s2] else [])
                            E("vector", lambda e, h=h, s4=s4, s2=s2: e.tensor_tensor(
                                sTm[s2][:, :], psc[:, s4 * 128 : (s4 + 1) * 128],
                                maskT[:, h * 128 : (h + 1) * 128], Alu.mult),
                              waits=w, inc=cV, n=1)
                            t_stm = TK(cV)
                            slot_sc[s4] = t_stm
                            w = t_qkv + [t_qdec] + ([qdb_free[s2]] if qdb_free[s2] else [])
                            E("vector", lambda e, b=b, h=h, hsl=hsl, s2=s2: e.tensor_tensor(
                                qdb[s2][:, :], qf[b][:, hsl],
                                qdec[:, h * 128 : (h + 1) * 128], Alu.mult),
                              waits=w, inc=cV, n=1)
                            t_qdb = TK(cV)
                            w = t_qkv + [t_ident] + ([slot_tr[s4]] if slot_tr[s4] else [])
                            E("tensor", lambda e, b=b, hsl=hsl, s4=s4: e.transpose(
                                _psb(ptrE, s4 * 128, (s4 + 1) * 128), kf[b][:, hsl],
                                ident_b[:, :]), waits=w, inc=cT, n=1)
                            t_ktr = TK(cT)
                            w = [t_ktr, t_dec2] + ([kdt_free[s2]] if kdt_free[s2] else [])
                            E("vector", lambda e, h=h, s4=s4, s2=s2: e.tensor_scalar(
                                kdt[s2][:, :], _psb(ptrE, s4 * 128, (s4 + 1) * 128),
                                kdcol[:, h : h + 1], None, Alu.mult), waits=w, inc=cV, n=1)
                            t_kdt = TK(cV)
                            slot_tr[s4] = t_kdt
                            w = [t_stm] + t_qkv + ([slot_o[s4]] if slot_o[s4] else [])
                            E("tensor", lambda e, b=b, hsl=hsl, s4=s4, s2=s2: e.matmul(
                                po[:, s4 * 128 : (s4 + 1) * 128], vf[b][:, hsl], sTm[s2][:, :],
                                start=True, stop=False), waits=w, inc=cT, n=1)
                            E("tensor", lambda e, hsl=hsl, s4=s4, s2=s2: e.matmul(
                                po[:, s4 * 128 : (s4 + 1) * 128], S[:, hsl], qdb[s2][:, :],
                                start=False, stop=True),
                              waits=[t_qdb, t_Supd_h.get(h, t_Supd)], inc=cT, n=1)
                            t_omm = TK(cT)
                            sTm_free[s2] = t_omm
                            qdb_free[s2] = t_omm
                            w = [t_kdt] + t_qkv + ([slot_a[s4]] if slot_a[s4] else [])
                            E("tensor", lambda e, b=b, hsl=hsl, s4=s4, s2=s2: e.matmul(
                                pa[:, s4 * 128 : (s4 + 1) * 128], kdt[s2][:, :], vf[b][:, hsl],
                                start=True, stop=True), waits=w, inc=cT, n=1)
                            t_amm = TK(cT)
                            kdt_free[s2] = t_amm
                            w = [t_omm] + ([Og_read] if Og_read else [])
                            E("vector", lambda e, h=h, tc4=tc4, s4=s4: e.tensor_copy(
                                Og[:, h * 512 + tc4 * 128 : h * 512 + (tc4 + 1) * 128],
                                po[:, s4 * 128 : (s4 + 1) * 128]), waits=w, inc=cV, n=1)
                            t_oev = TK(cV)
                            slot_o[s4] = t_oev
                            t_og = t_oev
                            E("vector", lambda e, hsl=hsl, h=h, s4=s4: e.scalar_tensor_tensor(
                                S[:, hsl], S[:, hsl], bdcol[:, h : h + 1],
                                pa[:, s4 * 128 : (s4 + 1) * 128], Alu.mult, Alu.add),
                              waits=[t_amm, t_omm], inc=cV, n=1)
                            t_Supd = TK(cV)
                            t_Supd_h[h] = t_Supd
                            slot_a[s4] = t_Supd
                            last_use = t_Supd
                        qkv_free[b] = last_use

                    for tcp in range(2):
                        x2_done = [None, None]
                        for och in range(2):
                            def opinit(e, och=och):
                                r = None
                                for tc2 in range(2):
                                    for oc2 in range(2):
                                        oc = och * 2 + oc2
                                        r = e.matmul(
                                            pop[:, (tc2 * 2 + oc2) * 512 : (tc2 * 2 + oc2 + 1) * 512],
                                            ones_row[0:1, 0:128],
                                            bout_row[0:1, oc * 512 : (oc + 1) * 512],
                                            start=True, stop=False)
                                return r
                            w = [t_bout, t_ones, t_og] + [t for t in pop_free if t]
                            E("tensor", opinit, waits=w, inc=cT, n=1)
                            for kc in range(16):
                                t_wo = E("sync", lambda e, kc=kc, och=och: e.dma_start(
                                    out=wo[kc % 2][:, :], in_=wout_bf(kc, och * 1024, 1024)),
                                  waits=[t_packagQ] + ([wo_use[kc % 2]] if wo_use[kc % 2] else []),
                                  inc=QL.nxt(), n=16)

                                def opmm(e, kc=kc, tcp=tcp):
                                    r = None
                                    for tc2 in range(2):
                                        tl = (tcp * 2 + tc2) * 128
                                        for oc2 in range(2):
                                            r = e.matmul(
                                                pop[:, (tc2 * 2 + oc2) * 512 : (tc2 * 2 + oc2 + 1) * 512],
                                                Og[:, kc * 512 + tl : kc * 512 + tl + 128],
                                                wo[kc % 2][:, oc2 * 512 : (oc2 + 1) * 512],
                                                start=False, stop=(kc == 15))
                                    return r
                                E("tensor", opmm, waits=[t_wo, t_og], inc=cT, n=1)
                                wo_use[kc % 2] = TK(cT)
                            t_ops = TK(cT)
                            Og_read = t_ops
                            for tc2 in range(2):
                                tcg = g * 4 + tcp * 2 + tc2
                                if och == 0:
                                    w = list(xe_free[tc2] or [])
                                    t_xe_d[tc2] = E("sync", lambda e, tcg=tcg, tc2=tc2:
                                      e.dma_start(
                                        out=xe2[tc2][:, :],
                                        in_=x_in[tcg * 128 : (tcg + 1) * 128, :]),
                                      waits=w, inc=QL.nxt(), n=16)
                                    xe_free[tc2] = None
                                t_xe = t_xe_d[tc2]
                                for oc2 in range(2):
                                    oc = och * 2 + oc2
                                    bk = slice((tc2 * 2 + oc2) * 512, (tc2 * 2 + oc2 + 1) * 512)
                                    osl = slice(oc * 512, (oc + 1) * 512)
                                    w = [t_ops, t_garep, t_xe]
                                    if och == 0 and x2b_free[tc2]:
                                        w.extend(x2b_free[tc2])
                                    E("vector", lambda e, bk=bk, osl=osl, tc2=tc2: e.tensor_tensor(
                                        eC_[:, 0:512], pop[:, bk], ga_rep[:, osl], Alu.mult),
                                      waits=w, inc=cV, n=1)
                                    E("vector", lambda e, bk=bk, osl=osl, tc2=tc2: e.tensor_tensor(
                                        x2b2[tc2][:, osl], eC_[:, 0:512], xe2[tc2][:, osl],
                                        Alu.add), inc=cV, n=1)
                                    pop_free[tc2 * 2 + oc2] = TK(cV)
                            x2_done[0 if och == 0 else 0] = TK(cV)
                        t_x2v = TK(cV)
                        for tc2 in range(2):
                            tcg = g * 4 + tcp * 2 + tc2
                            xe_free[tc2] = t_x2v
                            t_x2sp[tcg] = E("gpsimd", lambda e, tcg=tcg, tc2=tc2: e.dma_start(
                                out=x2sp_d[tcg * 128 : (tcg + 1) * 128, :],
                                in_=x2b2[tc2][:, :]), waits=[t_x2v], inc=GL.nxt(), n=16)
                            t_x2tap = None
                            if debug >= 3:
                                t_x2tap = E("gpsimd", lambda e, tcg=tcg, tc2=tc2: e.dma_start(
                                    out=dbg["x2"][tcg * 128 : (tcg + 1) * 128, :],
                                    in_=x2b2[tc2][:, :]), inc=cOg, n=16)
                            xb = x2b2[tc2]
                            E("vector", lambda e, xb=xb: e.reduce_sum(elna[:, :], xb[:, :], AX),
                              waits=[t_x2v], inc=cV, n=1)
                            t_sx = TK(cV)
                            E("scalar", lambda e, xb=xb: e.activation(
                                eC_[:, :], xb[:, :], Act.Square, accum_out=elnb[:, :]),
                              waits=[t_x2v] + (first_sc2 or []) +
                                    ([t_eCfree] if t_eCfree else []) + (t_q5p or []),
                              inc=cS, n=1)
                            first_sc2 = None
                            t_sq2 = TK(cS)
                            E("vector", lambda e: e.tensor_scalar(elnm[:, :], elna[:, :], 1.0 / D,
                                                                  None, Alu.mult),
                              waits=[t_sx, TK(cV)] + (t_elntap or []), inc=cV, n=1)
                            E("vector", lambda e: e.tensor_scalar(elnb[:, :], elnb[:, :], 1.0 / D,
                                                                  None, Alu.mult),
                              waits=[t_sq2, TK(cV)], inc=cV, n=1)
                            E("vector", lambda e: e.scalar_tensor_tensor(
                                elna[:, :], elnm[:, :], elnm[:, 0:1], elnb[:, :],
                                Alu.mult, Alu.subtract), waits=[TK(cV)], inc=cV, n=1)
                            E("vector", lambda e: e.tensor_scalar(elna[:, :], elna[:, :], -1.0,
                                                                  None, Alu.mult),
                              waits=[TK(cV)], inc=cV, n=1)
                            t_var = TK(cV)
                            E("scalar", lambda e: e.activation(elna[:, :], elna[:, :], Act.Sqrt,
                                                               bias=cp[:, CEPS : CEPS + 1]),
                              waits=[t_var], inc=cS, n=1)
                            t_std = TK(cS)
                            E("vector", lambda e: e.reciprocal(elnr[:, :], elna[:, :]),
                              waits=[t_std], inc=cV, n=1)
                            eA = xe2[tc2]
                            eB = xe2[1 - tc2]
                            E("vector", lambda e, xb=xb, eA=eA: e.tensor_scalar(
                                eA[:, :], xb[:, :], elnm[:, 0:1], elnr[:, 0:1],
                                Alu.subtract, Alu.mult),
                              waits=list(xe_free[1 - tc2] or []) + [TK(cV)] + (t_q5p or []),
                              inc=cV, n=1)
                            E("vector", lambda e, eA=eA: e.tensor_tensor(
                                eA[:, :], eA[:, :], cb1_rep[:, :], Alu.mult),
                              waits=[t_mods], inc=cV, n=1)
                            E("vector", lambda e, eA=eA: e.tensor_tensor(
                                eA[:, :], eA[:, :], sb_rep[:, :], Alu.add), inc=cV, n=1)
                            t_h2 = TK(cV)
                            t_h2tap = None
                            if debug >= 3:
                                t_h2tap = E("gpsimd", lambda e, tcg=tcg, eA=eA: e.dma_start(
                                    out=dbg["h2"][tcg * 128 : (tcg + 1) * 128, :],
                                    in_=eA[:, :]), waits=[t_h2], inc=cOg, n=16)
                                t1e = E("gpsimd", lambda e, tcg=tcg: e.dma_start(
                                    out=dbg["eln"][tcg, :, 0:1], in_=elnm[:, :],
                                    allow_slow_non_contiguous=True),
                                  waits=[t_h2], inc=cOg, n=16)
                                t2e = E("gpsimd", lambda e, tcg=tcg: e.dma_start(
                                    out=dbg["eln"][tcg, :, 1:2], in_=elnr[:, :],
                                    allow_slow_non_contiguous=True),
                                  waits=[t_h2], inc=cOg, n=16)
                                t_elntap = [t1e, t2e]
                            E("vector", lambda e, eA=eA: e.tensor_reduce(
                                emx[:, :].rearrange("p (b e) -> p b e", e=1), _blkv(eA, D), AX,
                                Alu.max, apply_absolute_value=True), inc=cV, n=1)
                            E("vector", lambda e: e.tensor_scalar(es_[:, :], emx[:, :], INV448,
                                                                  1e-30, Alu.mult, Alu.max),
                              waits=[TK(cV)] + (t_q5p or []), inc=cV, n=1)
                            E("vector", lambda e: e.reciprocal(ers[:, :], es_[:, :]),
                              waits=[TK(cV)], inc=cV, n=1)
                            E("vector", lambda e, eA=eA, eB=eB: e.tensor_tensor(
                                _blkv(eB, D), _blkv(eA, D), _bcv(ers, D), Alu.mult),
                              waits=[TK(cV)], inc=cV, n=1)
                            t_y = TK(cV)
                            if debug >= 5:
                                t_q5.append(E("gpsimd", lambda e, tcg=tcg, eB=eB: e.dma_start(
                                    out=dbg["ey"][tcg * 128 : (tcg + 1) * 128, :], in_=eB[:, :]),
                                  waits=[t_y], inc=cOg, n=16))
                            E("scalar", lambda e, eB=eB: e.activation(eC_[:, :], eB[:, :],
                                                                      Act.Square),
                              waits=[t_y] + (t_q5p or []), inc=cS, n=1)
                            t_sq3 = TK(cS)
                            E("vector", lambda e: e.tensor_scalar(eC_[:, :], eC_[:, :],
                                                                  0.000244140625, None, Alu.max),
                              waits=[t_sq3], inc=cV, n=1)
                            t_cl = TK(cV)
                            E("scalar", lambda e: e.activation(eC_[:, :], eC_[:, :], Act.Ln),
                              waits=[t_cl], inc=cS, n=1)
                            t_ln = TK(cS)
                            E("vector", lambda e: e.tensor_scalar(eEi[:, :], eC_[:, :], INV_2LN2,
                                                                  -0.5, Alu.mult, Alu.add),
                              waits=[t_ln] + (t_q5p or []), inc=cV, n=1)
                            t_ei = TK(cV)
                            t_eetap = None
                            if debug >= 5:
                                t_eetap = E("gpsimd", lambda e, tcg=tcg: e.dma_start(
                                    out=dbg["ee"][tcg * 128 : (tcg + 1) * 128, :], in_=eEi[:, :]),
                                  waits=[t_ei], inc=cOg, n=16)
                                t_q5.append(t_eetap)
                            E("scalar", lambda e: e.activation(eC_[:, :], eEi[:, :], Act.Exp,
                                                               bias=cp[:, CNB3 : CNB3 + 1],
                                                               scale=cp[:, CLN2 : CLN2 + 1]),
                              waits=[t_ei], inc=cS, n=1)
                            E("scalar", lambda e, eA=eA: e.activation(eA[:, :], eEi[:, :], Act.Exp,
                                                                      bias=cp[:, CPB3 : CPB3 + 1],
                                                                      scale=cp[:, CNLN2 : CNLN2 + 1]),
                              waits=([t_h2tap] if t_h2tap else []), inc=cS, n=1)
                            t_exp = TK(cS)
                            if debug >= 5:
                                t_q5.append(E("gpsimd", lambda e, tcg=tcg: e.dma_start(
                                    out=dbg["eu"][tcg * 128 : (tcg + 1) * 128, :], in_=eC_[:, :]),
                                  waits=[t_exp], inc=cOg, n=16))
                            E("vector", lambda e, eA=eA, eB=eB: e.tensor_tensor(
                                eA[:, :], eB[:, :], eA[:, :], Alu.mult), waits=[t_exp],
                              inc=cV, n=1)
                            t_z5 = TK(cV)
                            t_eztap = None
                            if debug >= 5:
                                t_eztap = E("gpsimd", lambda e, tcg=tcg, eA=eA: e.dma_start(
                                    out=dbg["ez"][tcg * 128 : (tcg + 1) * 128, :], in_=eA[:, :]),
                                  waits=[t_z5], inc=cOg, n=16)
                                t_q5.append(t_eztap)
                            E("vector", lambda e, eA=eA: e.tensor_copy(eEi[:, :], eA[:, :]),
                              waits=([t_eetap] if t_eetap else []), inc=cV, n=1)
                            t_r5 = TK(cV)
                            if debug >= 5:
                                t_q5.append(E("gpsimd", lambda e, tcg=tcg: e.dma_start(
                                    out=dbg["er"][tcg * 128 : (tcg + 1) * 128, :], in_=eEi[:, :]),
                                  waits=[t_r5], inc=cOg, n=16))
                            E("vector", lambda e, eA=eA: e.tensor_tensor(
                                eA[:, :], eEi[:, :], eC_[:, :], Alu.mult),
                              waits=([t_eztap] if t_eztap else []), inc=cV, n=1)
                            t_eCfree = TK(cV)
                            E("vector", lambda e, eA=eA: e.tensor_tensor(
                                _blkv(x2qtm, D), _blkv(eA, D), _bcv(es_, D), Alu.mult),
                              waits=([t_x2qtap] if t_x2qtap else []), inc=cV, n=1)
                            t_dqx = TK(cV)
                            if debug >= 5:
                                t_q5.append(E("gpsimd", lambda e, tcg=tcg: e.dma_start(
                                    out=dbg["es"][tcg, :, :], in_=es_[:, :]),
                                  waits=[t_dqx], inc=cOg, n=16))
                            xe_free[tc2] = [t_dqx] + list(t_q5)
                            xe_free[1 - tc2] = [t_dqx] + list(t_q5)
                            t_q5p = list(t_q5)
                            t_q5 = []
                            x2b_free[tc2] = [t_dqx, t_x2sp[tcg]]
                            if debug >= 3:
                                t_x2qtap = E("gpsimd", lambda e, tcg=tcg: e.dma_start(
                                    out=dbg["x2q"][0:16, :, tcg * 128 : (tcg + 1) * 128]
                                    .rearrange("a p t -> p a t"),
                                    in_=x2qtm[:, :].rearrange("p (a t) -> p a t", a=16)),
                                  waits=[t_dqx], inc=cOg, n=16)
                            for m in range(16):
                                s4 = 0
                                w = [t_dqx, t_ident] + ([slot_tr[s4]] if slot_tr[s4] else [])
                                E("tensor", lambda e, m=m, s4=s4: e.transpose(
                                    _psb(ptrE, s4 * 128, (s4 + 1) * 128),
                                    x2qtm[:, m * 128 : (m + 1) * 128], ident_b[:, :]),
                                  waits=w, inc=cT, n=1)
                                t_tr = TK(cT)
                                E("vector", lambda e, m=m, s4=s4, tcg=tcg: e.tensor_copy(
                                    x2q_sl(m, tcg * 128, (tcg + 1) * 128),
                                    _psb(ptrE, s4 * 128, (s4 + 1) * 128)),
                                  waits=[t_tr], inc=cV, n=1)
                                slot_tr[s4] = TK(cV)
                            x2b_free[tc2] = [TK(cV), t_x2sp[tcg]] + \
                                ([t_x2tap] if t_x2tap else [])
                t_x2q_last = TK(cV)
                barE = bar()

            if stage == "E":
                E("sync", lambda e: e.dma_start(out=out_d[:, :], in_=x2sp_d[:, :]),
                  waits=barE, inc=cO, n=16)
                finish([])
                return nc

            with ExitStack() as pg:
                gsb = lambda n, sh, dt=F32: pg.enter_context(nc.sbuf_tensor(n, sh, dt))
                gT = gsb("ggT", [128, 16 * 512], BF16)
                uT = gsb("guT", [128, 16 * 512], BF16)
                gs = gsb("ggs", [128, D])
                hb = gsb("ghb", [128, D])
                htm = gsb("ghtm", [128, D])
                htq = gsb("ghtq", [128, D], BF16)
                hqf = gsb("ghqf", [128, D], BF16)
                gB = gsb("ggB", [128, D])
                gC = gsb("ggC", [128, D])
                gEi = gsb("gEi", [128, D], I32)
                gmx = gsb("ggmx", [128, 64])
                gss = gsb("ggss", [128, 64])
                grs = gsb("ggrs", [128, 64])
                pgg = pg.enter_context(nc.psum_tensor("pgg", [128, 4 * 512], F32))
                pgu = pg.enter_context(nc.psum_tensor("pgu", [128, 4 * 512], F32))

                gu_use = None
                ptr_free = [None] * 4
                hqf_free = None
                first = barE

                for m4 in range(16):
                    w = list(first or []) + ([gu_use] if gu_use else []) + [t_packag]
                    first = None
                    t_w = []
                    for kc in range(16):
                        rnk = (m4 * 512) // GSH
                        roff = (m4 * 512) % GSH
                        t_w.append(E("sync", lambda e, kc=kc, rnk=rnk, roff=roff: e.dma_start(
                            out=gT[:, kc * 512 : (kc + 1) * 512],
                            in_=pk2d(rnk, OFF_G + roff * D, 512, D)[:, kc * 128 : (kc + 1) * 128],
                            transpose=True), waits=w, inc=QL.nxt(), n=16))
                        w = []
                        t_w.append(E("sync", lambda e, kc=kc, rnk=rnk, roff=roff: e.dma_start(
                            out=uT[:, kc * 512 : (kc + 1) * 512],
                            in_=pk2d(rnk, OFF_U + roff * D, 512, D)[:, kc * 128 : (kc + 1) * 128],
                            transpose=True), inc=QL.nxt(), n=16))
                    for mi in range(4):
                        m = m4 * 4 + mi
                        msl = slice(mi * 128, (mi + 1) * 128)

                        def gumm(e, msl=msl):
                            r = None
                            for kc in range(16):
                                for ts in range(4):
                                    r = e.matmul(pgg[:, ts * 512 : (ts + 1) * 512],
                                                 gT[:, kc * 512 : (kc + 1) * 512][:, msl],
                                                 x2q_sl(kc, ts * 512, (ts + 1) * 512),
                                                 start=(kc == 0), stop=(kc == 15))
                                for ts in range(4):
                                    r = e.matmul(pgu[:, ts * 512 : (ts + 1) * 512],
                                                 uT[:, kc * 512 : (kc + 1) * 512][:, msl],
                                                 x2q_sl(kc, ts * 512, (ts + 1) * 512),
                                                 start=(kc == 0), stop=(kc == 15))
                            return r
                        w2 = t_w + [t_x2q_last] + [t for t in ptr_free if t]
                        E("tensor", gumm, waits=w2, inc=cT, n=1)
                        t_gu = TK(cT)
                        E("scalar", lambda e: e.activation(gs[:, :], pgg[:, :], Act.Silu),
                          waits=[t_gu], inc=cS, n=1)
                        t_silu = TK(cS)
                        E("vector", lambda e: e.tensor_tensor(hb[:, :], gs[:, :], pgu[:, :],
                                                              Alu.mult), waits=[t_silu],
                          inc=cV, n=1)
                        t_h = TK(cV)
                        for t16 in range(16):
                            s4 = t16 % 4
                            w3 = [t_h, t_ident] + ([ptr_free[s4]] if ptr_free[s4] else [])
                            E("tensor", lambda e, t16=t16, s4=s4: e.transpose(
                                pgg[:, s4 * 512 : s4 * 512 + 128],
                                hb[:, t16 * 128 : (t16 + 1) * 128], ident_f[:, :]),
                              waits=w3, inc=cT, n=1)
                            t_tr = TK(cT)
                            E("vector", lambda e, t16=t16, s4=s4: e.tensor_copy(
                                htm[:, t16 * 128 : (t16 + 1) * 128],
                                pgg[:, s4 * 512 : s4 * 512 + 128]), waits=[t_tr], inc=cV, n=1)
                            ptr_free[s4] = TK(cV)
                        t_htm = TK(cV)
                        E("vector", lambda e: e.tensor_reduce(
                            gmx[:, :].rearrange("p (b e) -> p b e", e=1), _blkv(htm, D), AX,
                            Alu.max, apply_absolute_value=True), waits=[t_htm], inc=cV, n=1)
                        E("vector", lambda e: e.tensor_scalar(gss[:, :], gmx[:, :], INV448, 1e-30,
                                                              Alu.mult, Alu.max),
                          waits=[TK(cV)], inc=cV, n=1)
                        E("vector", lambda e: e.reciprocal(grs[:, :], gss[:, :]),
                          waits=[TK(cV)], inc=cV, n=1)
                        E("vector", lambda e: e.tensor_tensor(_blkv(gB, D), _blkv(htm, D),
                                                              _bcv(grs, D), Alu.mult),
                          waits=[TK(cV)], inc=cV, n=1)
                        t_y = TK(cV)
                        E("scalar", lambda e: e.activation(gC[:, :], gB[:, :], Act.Square),
                          waits=[t_y], inc=cS, n=1)
                        t_sq = TK(cS)
                        E("vector", lambda e: e.tensor_scalar(gC[:, :], gC[:, :], 0.000244140625,
                                                              None, Alu.max), waits=[t_sq],
                          inc=cV, n=1)
                        t_cl = TK(cV)
                        E("scalar", lambda e: e.activation(gC[:, :], gC[:, :], Act.Ln),
                          waits=[t_cl], inc=cS, n=1)
                        t_ln2 = TK(cS)
                        E("vector", lambda e: e.tensor_scalar(gEi[:, :], gC[:, :], INV_2LN2, -0.5,
                                                              Alu.mult, Alu.add),
                          waits=[t_ln2], inc=cV, n=1)
                        t_ei = TK(cV)
                        E("scalar", lambda e: e.activation(gC[:, :], gEi[:, :], Act.Exp,
                                                           bias=cp[:, CNB3 : CNB3 + 1],
                                                           scale=cp[:, CLN2 : CLN2 + 1]),
                          waits=[t_ei], inc=cS, n=1)
                        E("scalar", lambda e: e.activation(htm[:, :], gEi[:, :], Act.Exp,
                                                           bias=cp[:, CPB3 : CPB3 + 1],
                                                           scale=cp[:, CNLN2 : CNLN2 + 1]),
                          inc=cS, n=1)
                        t_exp = TK(cS)
                        E("vector", lambda e: e.tensor_tensor(htm[:, :], gB[:, :], htm[:, :],
                                                              Alu.mult), waits=[t_exp],
                          inc=cV, n=1)
                        E("vector", lambda e: e.tensor_copy(gEi[:, :], htm[:, :]), inc=cV, n=1)
                        E("vector", lambda e: e.tensor_tensor(htm[:, :], gEi[:, :], gC[:, :],
                                                              Alu.mult), inc=cV, n=1)
                        E("vector", lambda e: e.tensor_tensor(_blkv(htq, D), _blkv(htm, D),
                                                              _bcv(gss, D), Alu.mult),
                          inc=cV, n=1)
                        t_htq = TK(cV)
                        for t16 in range(16):
                            s4 = t16 % 4
                            w3 = [t_htq, t_ident] + ([ptr_free[s4]] if ptr_free[s4] else [])
                            E("tensor", lambda e, t16=t16, s4=s4: e.transpose(
                                _psb(pgu, s4 * 512, s4 * 512 + 128),
                                htq[:, t16 * 128 : (t16 + 1) * 128], ident_b[:, :]),
                              waits=w3, inc=cT, n=1)
                            t_tr = TK(cT)
                            w4 = [t_tr] + ([hqf_free] if hqf_free else [])
                            E("vector", lambda e, t16=t16, s4=s4: e.tensor_copy(
                                hqf[:, t16 * 128 : (t16 + 1) * 128],
                                _psb(pgu, s4 * 512, s4 * 512 + 128)), waits=w4, inc=cV, n=1)
                            ptr_free[s4] = TK(cV)
                        t_hqf = TK(cV)
                        hqf_free = E("gpsimd", lambda e, m=m: e.dma_start(out=hq_d[m, :, :],
                                                                          in_=hqf[:, :]),
                                     waits=[t_hqf], inc=GL.nxt(), n=16)
                        gu_use = t_gu if mi < 3 else TK(cT)
                    gu_use = TK(cT)
                t_hq_all = GL.cur()
                if debug >= 4:
                    E("gpsimd", lambda e: e.dma_start(out=dbg["hq"][:, :, :], in_=hq_d[:, :, :]),
                      waits=t_hq_all, inc=cOg, n=16)
                barG1 = bar()

        with ExitStack() as pgd:
            dsb = lambda n, sh, dt=F32: pgd.enter_context(nc.sbuf_tensor(n, sh, dt))
            outst = dsb("dout", [128, 8 * D])
            fmt = dsb("dfmt", [128, 8 * 512], BF16)
            hqr = [dsb(f"dhqr{i}", [128, 1024], BF16) for i in range(2)]
            ddT = [dsb(f"dddT{i}", [128, 512], BF16) for i in range(2)]
            x2r = dsb("dx2r", [128, D])
            pdn = pgd.enter_context(nc.psum_tensor("pdn", [128, 8 * 512], F32))

            hq_use = [None, None]
            dd_use = [None, None]
            fmt_free = None
            outst_free = [None] * 8
            pdnrd = [None] * 4
            x2r_free = None
            first = barG1
            for tsp in range(2):
                out_wr = []
                for oq in range(4):
                    t_mm = None
                    for m in range(64):
                        b = m % 2
                        w = list(first or []) + [t_packag] + \
                            ([dd_use[b]] if dd_use[b] else [])
                        first = None
                        t_d1 = E("sync", lambda e, m=m, oq=oq, b=b: e.dma_start(
                            out=ddT[b][:, 0:256],
                            in_=pk2d(2 * oq, OFF_DW, DSH, IF)[:, m * 128 : (m + 1) * 128],
                            transpose=True), waits=w, inc=QL.nxt(), n=16)
                        t_d2 = E("sync", lambda e, m=m, oq=oq, b=b: e.dma_start(
                            out=ddT[b][:, 256:512],
                            in_=pk2d(2 * oq + 1, OFF_DW, DSH, IF)[:, m * 128 : (m + 1) * 128],
                            transpose=True), inc=QL.nxt(), n=16)
                        w = list(t_hq_all) + ([hq_use[b]] if hq_use[b] else [])
                        t_d3 = E("sync", lambda e, m=m, tsp=tsp, b=b: e.dma_start(
                            out=hqr[b][:, :], in_=hq_d[m, :, tsp * 1024 : (tsp + 1) * 1024]),
                          waits=w, inc=QL.nxt(), n=16)
                        t_ld = [t_d1, t_d2, t_d3]

                        def dnmm(e, m=m, b=b):
                            r = None
                            for oc in range(4):
                                for ts2 in range(2):
                                    r = e.matmul(
                                        pdn[:, (oc * 2 + ts2) * 512 : (oc * 2 + ts2 + 1) * 512],
                                        ddT[b][:, oc * 128 : (oc + 1) * 128],
                                        hqr[b][:, ts2 * 512 : (ts2 + 1) * 512],
                                        start=(m == 0), stop=(m == 63))
                            return r
                        w = list(t_ld)
                        if m == 0 and fmt_free:
                            w.append(fmt_free)
                        E("tensor", dnmm, waits=w, inc=cT, n=1)
                        t_mm = TK(cT)
                        hq_use[b] = t_mm
                        dd_use[b] = t_mm
                    ev = []
                    for oc in range(4):
                        for ts2 in range(2):
                            bk = slice((oc * 2 + ts2) * 512, (oc * 2 + ts2 + 1) * 512)
                            E("vector", lambda e, bk=bk, oq=oq, oc=oc: e.tensor_scalar(
                                fmt[:, bk], pdn[:, bk], gbcol[:, oq * 4 + oc : oq * 4 + oc + 1],
                                None, Alu.mult), waits=[t_mm, t_geff], inc=cV, n=1)
                            ev.append(TK(cV))
                    t_ev = TK(cV)
                    fmt_free = None
                    for oc in range(4):
                        for ts2 in range(2):
                            for tt in range(4):
                                tch = tsp * 8 + ts2 * 4 + tt
                                s4 = tt
                                w = [t_ev, t_ident] + \
                                    ([pdnrd[s4]] if pdnrd[s4] else [])
                                E("tensor", lambda e, oc=oc, ts2=ts2, tt=tt, s4=s4: e.transpose(
                                    _psb(pdn, s4 * 512, s4 * 512 + 128),
                                    fmt[:, (oc * 2 + ts2) * 512 + tt * 128 :
                                        (oc * 2 + ts2) * 512 + (tt + 1) * 128],
                                    ident_b[:, :]), waits=w, inc=cT, n=1)
                                t_tr = TK(cT)
                                oco = oq * 4 + oc
                                w = [t_tr]
                                if outst_free[ts2 * 4 + tt]:
                                    w.append(outst_free[ts2 * 4 + tt])
                                E("vector", lambda e, tch=tch, oco=oco, s4=s4, ts2=ts2, tt=tt:
                                  e.tensor_copy(
                                    outst[:, (ts2 * 4 + tt) * D + oco * 128 :
                                          (ts2 * 4 + tt) * D + (oco + 1) * 128],
                                    _psb(pdn, s4 * 512, s4 * 512 + 128)), waits=w, inc=cV, n=1)
                                pdnrd[s4] = TK(cV)
                    fmt_free = TK(cV)
                t_alltr = TK(cV)
                for tl in range(8):
                    tcg = tsp * 8 + tl
                    w = [t_x2sp.get(tcg), t_alltr] + ([x2r_free] if x2r_free else [])
                    t_xr = E("sync", lambda e, tcg=tcg: e.dma_start(
                        out=x2r[:, :], in_=x2sp_d[tcg * 128 : (tcg + 1) * 128, :]),
                      waits=[t for t in w if t], inc=QL.nxt(), n=16)
                    E("vector", lambda e, tl=tl: e.tensor_tensor(
                        outst[:, tl * D : (tl + 1) * D], outst[:, tl * D : (tl + 1) * D],
                        x2r[:, :], Alu.add), waits=[t_xr], inc=cV, n=1)
                    t_add = TK(cV)
                    x2r_free = t_add
                    out_wr.append(E("sync", lambda e, tcg=tcg, tl=tl: e.dma_start(
                        out=out_d[tcg * 128 : (tcg + 1) * 128, :],
                        in_=outst[:, tl * D : (tl + 1) * D]), waits=[t_add],
                        inc=QL.nxt(), n=16))
                    outst_free[tl] = out_wr[-1]
                for i in range(8):
                    outst_free[i] = out_wr[i] if tsp == 0 else None

        finish([])
        return nc


_NC_CACHE = {}


def kernel(L0_attn_x, L0_attn_Wqkv, L0_attn_bqkv, L0_attn_Wout, L0_attn_bout,
           L0_attn_slopes, L0_ffn_gate_w, L0_ffn_up_w, L0_ffn_down_w, L0_c, L0_wmod,
           _stage="full", _debug=0):
    x = np.ascontiguousarray(np.asarray(L0_attn_x, dtype=np.float32).reshape(NCORE * T, D))
    wqkv = np.ascontiguousarray(np.asarray(L0_attn_Wqkv, dtype=np.float32))
    bqkv = np.ascontiguousarray(np.asarray(L0_attn_bqkv, dtype=np.float32).reshape(1, 3 * D))
    wout = np.ascontiguousarray(np.asarray(L0_attn_Wout, dtype=np.float32))
    bout = np.ascontiguousarray(np.asarray(L0_attn_bout, dtype=np.float32).reshape(1, D))
    slopes = np.ascontiguousarray(np.asarray(L0_attn_slopes, dtype=np.float32).reshape(1, H))
    gw = np.ascontiguousarray(np.asarray(L0_ffn_gate_w, dtype=np.float32))
    uw = np.ascontiguousarray(np.asarray(L0_ffn_up_w, dtype=np.float32))
    dw = np.ascontiguousarray(np.asarray(L0_ffn_down_w, dtype=np.float32))
    c = np.ascontiguousarray(np.asarray(L0_c, dtype=np.float32).reshape(1, D))
    wmod = np.ascontiguousarray(np.asarray(L0_wmod, dtype=np.float32))

    in_maps = []
    for i in range(NCORE):
        steps = np.zeros((1, NCORE), dtype=np.float32)
        maskr = np.zeros((1, NCORE), dtype=np.float32)
        for j in range(NCORE):
            if j < i:
                steps[0, j] = float(i - 1 - j)
                maskr[0, j] = 1.0
        in_maps.append({
            "x": x[i * T : (i + 1) * T, :],
            "wqkv_sh": wqkv[i * QSH : (i + 1) * QSH, :],
            "bqkv": bqkv,
            "wout_sh": wout[i * WOSH : (i + 1) * WOSH, :],
            "bout": bout,
            "slopes": slopes,
            "g_sh": gw[i * GSH : (i + 1) * GSH, :],
            "u_sh": uw[i * GSH : (i + 1) * GSH, :],
            "dw_sh": dw[i * DSH : (i + 1) * DSH, :],
            "c_in": c,
            "wmod_sh": np.ascontiguousarray(wmod[:, i * WMSH : (i + 1) * WMSH]),
            "steps": steps,
            "maskr": maskr,
        })

    key = (_stage, _debug)
    if key not in _NC_CACHE:
        _NC_CACHE[key] = build(stage=_stage, debug=_debug)
    nc = _NC_CACHE[key]
    res = run_bass_kernel_spmd(nc, in_maps, core_ids=list(range(NCORE)))
    out = np.concatenate([np.asarray(res.results[i]["out"]) for i in range(NCORE)], axis=0)
    out = out.reshape(1, NCORE * T, D)
    if _debug:
        return out, res
    return out
